# version 07
# baseline (speedup 1.0000x reference)
import numpy as np
import ml_dtypes
import concourse.bass as bass
import concourse.mybir as mybir
import concourse.bass_utils as _bass_utils
from concourse.bass_utils import run_bass_kernel_spmd

_orig_run_command = _bass_utils.run_command

def _run_command_ldwopt(cmd, *a, **kw):
    if isinstance(cmd, list):
        cmd = ["--enable-ldw-opt=true" if c == "--enable-ldw-opt=false" else c
               for c in cmd]
    return _orig_run_command(cmd, *a, **kw)

_bass_utils.run_command = _run_command_ldwopt

F32 = mybir.dt.float32
BF16 = mybir.dt.bfloat16
AF = mybir.ActivationFunctionType
ALU = mybir.AluOpType
AX = mybir.AxisListType

B, T, D, DI, DS, DTR, HF = 8, 4096, 1024, 2048, 16, 64, 4096
TC = 512
NSEG = T // TC
CH = 128
NCH = TC // CH
TB = 1024
NTB = T // TB
ALPHA = 0.1
EPS = 1e-5

DT_K = D // 128
DI_K = DI // 128
HF_K = HF // 128


class Prog:
    ENGS = ("pe", "act", "dve", "pool", "sp")

    def __init__(self):
        self.phase_marks = []
        self.ops = {e: [] for e in self.ENGS}
        self.cnt = {e: 0 for e in self.ENGS}
        self.dma_cnt = {}
        self.last_wait = {}
        self.sem_names = set(self.ENGS)

    def op(self, e, fn):
        self.ops[e].append([fn, []])

    def mark(self, e):
        assert self.ops[e], f"mark on empty stream {e}"
        self.cnt[e] += 1
        self.ops[e][-1][1].append((e, 1))
        return self.cnt[e]

    def dma(self, queue, fn, sem_name):
        self.sem_names.add(sem_name)
        self.dma_cnt[sem_name] = self.dma_cnt.get(sem_name, 0) + 16
        self.ops[queue].append([fn, [(sem_name, 16)]])
        return self.dma_cnt[sem_name]

    def wait(self, consumer, sem_name, count):
        if count is None:
            return
        k = (consumer, sem_name)
        if self.last_wait.get(k, -1) >= count:
            return
        self.last_wait[k] = count
        self.ops[consumer].append([("WAIT", sem_name, count), []])

    def emit(self, nc, block, sems):
        handles = {"pe": nc.tensor, "act": nc.scalar, "dve": nc.vector,
                   "pool": nc.gpsimd, "sp": nc.sync}
        deco = {"pe": block.tensor, "act": block.scalar, "dve": block.vector,
                "pool": block.gpsimd, "sp": block.sync}
        for e in self.ENGS:
            ops = self.ops[e]
            eng = handles[e]

            def section(_eng=eng, _ops=ops):
                for fn, incs in _ops:
                    if isinstance(fn, tuple) and fn[0] == "WAIT":
                        _eng.wait_ge(sems[fn[1]], fn[2])
                        continue
                    inst = fn(_eng)
                    for (sem_name, step) in incs:
                        inst.then_inc(sems[sem_name], step)

            deco[e](section)


def build_nc(dbg=(), stop_after=None):
    import contextlib
    nc = bass.Bass("TRN2", target_bir_lowering=False)
    P = Prog()
    stack = contextlib.ExitStack()

    def din(name, shape, dt=F32):
        return nc.dram_tensor(name, shape, dt, kind="ExternalInput").ap()

    xT = din("xT", [D, T])
    ipw = din("ipw", [D, 2 * DI])
    xpw = din("xpw", [DI, 96])
    dtw = din("dtw", [DTR, DI])
    opw = din("opw", [DI, D])
    wwt = din("wwt", [D, D])
    wrt = din("wrt", [D, D])
    w1t = din("w1t", [D, HF])
    w2t = din("w2t", [D, HF])
    h1t = din("h1t", [HF, HF])
    h2t = din("h2t", [HF, HF])
    w3t = din("w3t", [HF, D])
    ipb_h = din("ipb_h", [128, 32])
    convw_h = din("convw_h", [128, 64])
    convb_h = din("convb_h", [128, 16])
    dtb_h = din("dtb_h", [128, 16])
    Alog_h = din("Alog_h", [128, 256])
    Dssm_h = din("Dssm_h", [128, 16])
    opb_h = din("opb_h", [128, 8])
    n1_h = din("n1_h", [128, 8])
    n2_h = din("n2_h", [128, 8])
    w1s_h = din("w1s_h", [128, 32])
    w2s_h = din("w2s_h", [128, 32])
    h1s_h = din("h1s_h", [128, 32])
    h2s_h = din("h2s_h", [128, 32])
    w3s_h = din("w3s_h", [128, 8])
    decay_h = din("decay_h", [1, 1])
    ones1_h = din("ones1_h", [1, 128])
    onesk_h = din("onesk_h", [128, 1])
    iota_h = din("iota_h", [128, 1])
    iotar_h = din("iotar_h", [128, 1])
    midx_h = din("midx_h", [128, 128])
    caus_h = din("caus_h", [128, 128])
    eye_h = din("eye_h", [128, 128])
    eyebf_h = din("eyebf_h", [128, 128], BF16)
    selB_h = din("selB_h", [32, 16 * 128], BF16)
    selC_h = din("selC_h", [32, 16 * 128], BF16)
    cvals_h = din("cvals_h", [128, 8])

    yT = nc.dram_tensor("yT", [D, T], F32, kind="ExternalOutput").ap()

    def scratch(name, shape, dt):
        return nc.dram_tensor(name, shape, dt, kind="Internal").ap()

    ipw_bf = scratch("ipw_bf", [D, 2 * DI], BF16)
    xpw_bf_d = scratch("xpw_bf_d", [DI, 96], BF16)
    dtw_bf_d = scratch("dtw_bf_d", [DTR, DI], BF16)
    opw_bf = scratch("opw_bf", [DI, D], BF16)
    ww_bf = scratch("ww_bf", [D, D], BF16)
    wr_bf = scratch("wr_bf", [D, D], BF16)
    w1_bf = scratch("w1_bf", [D, HF], BF16)
    w2_bf = scratch("w2_bf", [D, HF], BF16)
    h1_bf = scratch("h1_bf", [HF, HF], BF16)
    h2_bf = scratch("h2_bf", [HF, HF], BF16)
    w3_bf = scratch("w3_bf", [HF, D], BF16)
    x1_sp = scratch("x1_sp", [D, T], F32)
    out2_sp = scratch("out2_sp", [D, T], F32)
    xA_sp = scratch("xA_sp", [D, T], F32)
    xm_sp = scratch("xm_sp", [DI, T], BF16)
    sil_sp = scratch("sil_sp", [DI, T], BF16)

    dbg_srcs = {"x1": x1_sp, "out2": out2_sp, "xA": xA_sp,
                "xm": xm_sp, "sil": sil_sp}
    dbg_outs = {}
    for name in dbg:
        src = dbg_srcs[name]
        dbg_outs[name] = nc.dram_tensor(
            "dbg_" + name, list(src.shape), src.dtype,
            kind="ExternalOutput").ap()

    def mksb(st, name, shape, dt):
        return st.enter_context(nc.sbuf_tensor(name, shape, dt))

    def sbuf(name, shape, dt):
        return mksb(stack, name, shape, dt)

    IPB = sbuf("IPB", [128, 32], F32)
    CONVW = sbuf("CONVW", [128, 64], F32)
    CONVB = sbuf("CONVB", [128, 16], F32)
    DTB = sbuf("DTB", [128, 16], F32)
    ASB = sbuf("ASB", [128, 256], F32)
    DSSM = sbuf("DSSM", [128, 16], F32)
    OPB = sbuf("OPB", [128, 8], F32)
    N1W = sbuf("N1W", [128, 8], F32)
    N2W = sbuf("N2W", [128, 8], F32)
    FS1 = sbuf("FS1", [128, 32], F32)
    FS2 = sbuf("FS2", [128, 32], F32)
    FH1 = sbuf("FH1", [128, 32], F32)
    FH2 = sbuf("FH2", [128, 32], F32)
    FS3 = sbuf("FS3", [128, 8], F32)
    ONES1 = sbuf("ONES1", [1, 128], F32)
    ONESK = sbuf("ONESK", [128, 1], F32)
    IOTA = sbuf("IOTA", [128, 1], F32)
    IOTAR = sbuf("IOTAR", [128, 1], F32)
    MIDX = sbuf("MIDX", [128, 128], F32)
    CAUS = sbuf("CAUS", [128, 128], F32)
    EYE = sbuf("EYE", [128, 128], F32)
    EYEBF = sbuf("EYEBF", [128, 128], BF16)
    SELB = sbuf("SELB", [32, 16 * 128], BF16)
    SELC = sbuf("SELC", [32, 16 * 128], BF16)
    DEC = sbuf("DEC", [1, 4], F32)
    LNG = sbuf("LNG", [128, 1], F32)
    GIN = sbuf("GIN", [128, 1], F32)
    GW = sbuf("GW", [128, 1], F32)
    GC = sbuf("GC", [128, 1], F32)
    MT = sbuf("MT", [128, 128], F32)
    GINR = sbuf("GINR", [128, 128], F32)
    CVAL = sbuf("CVAL", [128, 8], F32)
    _CONST_VALS = [1.0, -1.0, -0.5, 128.0, 1.0 / D, EPS, ALPHA, 0.0]
    for _j, _v in enumerate(_CONST_VALS):
        nc.const_aps.register(F32, _v, CVAL[:, _j:_j + 1])
    SCAR = sbuf("SCAR", [128, 16 * 16], F32)
    CCAR = sbuf("CCAR", [128, 16 * 3], BF16)
    GROW = sbuf("GROW", [1, 128], F32)
    import contextlib as _ctx
    st_pro = _ctx.ExitStack()
    STG0 = mksb(st_pro, "STG0", [128, 2048], BF16)
    st_m = _ctx.ExitStack()

    def alloc_mamba():
        g = {}
        g["WS"] = mksb(st_m, "WS", [128, DT_K * 1024], BF16)
        g["XPW"] = mksb(st_m, "XPW", [128, DI_K * 96], BF16)
        g["DTW"] = mksb(st_m, "DTW", [64, DI], BF16)
        g["XSEG"] = mksb(st_m, "XSEG", [128, DT_K * TC], F32)
        g["H0B"] = mksb(st_m, "H0B", [128, DT_K * TC], BF16)
        g["XIN"] = mksb(st_m, "XIN", [128, DT_K * TC], BF16)
        g["SCR"] = mksb(st_m, "SCR", [128, 7 * 512], F32)
        g["BIL"] = mksb(st_m, "BIL", [128, TC * 16], BF16)
        g["CIL"] = mksb(st_m, "CIL", [128, TC * 16], BF16)
        g["DBUIL"] = mksb(st_m, "DBUIL", [128, TC * 16], BF16)
        g["HIL"] = mksb(st_m, "HIL", [128, TC * 16], F32)
        g["YBF"] = mksb(st_m, "YBF", [128, DI_K * TC], BF16)
        g["XMD"] = mksb(st_m, "XMD", [128, 2 * TC], BF16)
        g["SLD"] = mksb(st_m, "SLD", [128, 2 * TC], BF16)
        g["XDBL0"] = mksb(st_m, "XDBL0", [64, TC], BF16)
        g["XDBL1"] = mksb(st_m, "XDBL1", [64, TC], BF16)
        g["BCR"] = mksb(st_m, "BCR", [32, TC], BF16)
        g["MBX"] = mksb(st_m, "MBX", [128, TC], F32)
        g["SBX"] = mksb(st_m, "SBX", [128, TC], F32)
        g["CBX"] = mksb(st_m, "CBX", [128, TC], F32)
        g["XZM"] = mksb(st_m, "XZM", [128, TC + 3], BF16)
        g["CONVACC"] = mksb(st_m, "CONVACC", [128, TC], F32)
        g["FSCR"] = mksb(st_m, "FSCR", [128, TC], F32)
        g["OPSL"] = mksb(st_m, "OPSL", [128, 2 * TC], F32)
        g["XMST"] = g["FSCR"][:].bitcast(BF16)
        g["YPB"] = g["HIL"][:, 0:TC * 8].bitcast(BF16)
        S = g["SCR"]
        g["DELTA0"] = S[:, 0 * 512:1 * 512]
        g["DELTA1"] = S[:, 1 * 512:2 * 512]
        g["SPT"] = S[:, 2 * 512:3 * 512]
        g["DA0"] = S[:, 3 * 512:4 * 512]
        g["DA1"] = S[:, 4 * 512:5 * 512]
        g["YSSM"] = S[:, 5 * 512:6 * 512]
        g["DU0"] = S[:, 6 * 512:6 * 512 + 256].bitcast(BF16)
        g["DU1"] = S[:, 6 * 512 + 256:7 * 512].bitcast(BF16)
        return g

    ps = [stack.enter_context(nc.psum_tensor(f"ps{i}", [128, 512], F32))
          for i in range(7)]
    psbf = stack.enter_context(nc.psum_tensor("psbf", [128, 512], BF16))

    def MM(out_ap, lhsT_ap, rhs_ap, start, stop):
        P.op("pe", lambda t, o=out_ap, a=lhsT_ap, b=rhs_ap, s1=start, s2=stop:
             t.matmul(o, a, b, start=s1, stop=s2))

    def ACT(out_ap, in_ap, func, bias=None, scale=None):
        kw = {}
        if bias is not None:
            kw["bias"] = bias
        if scale is not None:
            kw["scale"] = scale
        P.op("act", lambda a, o=out_ap, i=in_ap, f=func, k=kw:
             a.activation(o, i, f, **k))

    def TT(out_ap, a_ap, b_ap, op):
        P.op("dve", lambda v, o=out_ap, x=a_ap, y=b_ap, p_=op:
             v.tensor_tensor(o, x, y, p_))

    def STT(out_ap, in0_ap, scal, in1_ap, op0, op1):
        P.op("dve", lambda v, o=out_ap, i0=in0_ap, s=scal, i1=in1_ap,
             p0=op0, p1=op1: v.scalar_tensor_tensor(o, i0, s, i1, p0, p1))

    cl = None
    for dst, src in [(IPB, ipb_h), (CONVW, convw_h), (CONVB, convb_h),
                     (DTB, dtb_h), (DSSM, Dssm_h), (OPB, opb_h),
                     (N1W, n1_h), (N2W, n2_h), (FS1, w1s_h), (FS2, w2s_h),
                     (FH1, h1s_h), (FH2, h2s_h), (FS3, w3s_h),
                     (ONES1, ones1_h), (ONESK, onesk_h), (IOTA, iota_h),
                     (IOTAR, iotar_h), (MIDX, midx_h), (CAUS, caus_h),
                     (EYE, eye_h), (EYEBF, eyebf_h), (SELB, selB_h),
                     (SELC, selC_h)]:
        cl = P.dma("sp", lambda e, d=dst, s=src: e.dma_start(out=d[:], in_=s),
                   "dma_a")
    cl = P.dma("sp", lambda e: e.dma_start(out=CVAL[:], in_=cvals_h), "dma_a")
    cl = P.dma("sp", lambda e: e.dma_start(out=DEC[0:1, 0:1], in_=decay_h),
               "dma_a")
    cl = P.dma("sp", lambda e: e.dma_start(out=ASB[:], in_=Alog_h), "dma_a")

    P.wait("act", "dma_a", cl)
    ACT(ASB[:], ASB[:], AF.Exp)
    ACT(ASB[:], ASB[:], AF.Copy, scale=-1.0)
    ACT(DEC[0:1, 1:2], DEC[0:1, 0:1], AF.Sigmoid)
    ACT(DEC[0:1, 2:3], DEC[0:1, 1:2], AF.Ln)
    mk_act = P.mark("act")
    P.wait("pe", "act", mk_act)
    MM(ps[0][:, 0:1], ONES1[:], DEC[0:1, 2:3], True, True)
    mk_pe = P.mark("pe")
    P.wait("act", "pe", mk_pe)
    ACT(LNG[:], ps[0][:, 0:1], AF.Copy)
    ACT(GIN[:], LNG[:], AF.Exp, scale=IOTA[:, 0:1])
    ACT(GW[:], LNG[:], AF.Exp, scale=IOTAR[:, 0:1])
    ACT(GC[:], LNG[:], AF.Exp, scale=128.0)
    ACT(MT[:], MIDX[:], AF.Exp, scale=LNG[:, 0:1])
    mk_act = P.mark("act")
    P.wait("dve", "act", mk_act)
    TT(MT[:], MT[:], CAUS[:], ALU.mult)
    P.wait("pe", "act", mk_act)
    MM(ps[1][0:1, 0:128], GIN[:], EYE[:], True, True)
    mk_pe = P.mark("pe")
    P.wait("act", "pe", mk_pe)
    ACT(GROW[0:1, 0:128], ps[1][0:1, 0:128], AF.Copy)
    mk_act = P.mark("act")
    P.wait("pe", "act", mk_act)
    MM(ps[1][:, 0:128], ONES1[:], GROW[0:1, 0:128], True, True)
    mk_pe = P.mark("pe")
    P.wait("act", "pe", mk_pe)
    ACT(GINR[:], ps[1][:, 0:128], AF.Copy)
    mk_ginr = P.mark("act")

    P.op("dve", lambda v: v.memset(SCAR[:], 0.0))
    P.op("dve", lambda v: v.memset(CCAR[:], 0.0))
    mk_zero = P.mark("dve")

    stage_last_store = {0: None, 1: None}
    cast_idx = [0]

    def cast_weight(src, dst, rows, cols):
        for r in range(0, rows, 128):
            rr = min(128, rows - r)
            for c0 in range(0, cols, 2048):
                cc = min(2048, cols - c0)
                stage = STG0
                P.wait("pool", "ds0", stage_last_store[0])
                ld = P.dma("pool",
                           lambda e, s=src, st=stage, r0=r, rr_=rr, cb=c0,
                           c=cc: e.dma_start(
                               out=st[0:rr_, 0:c],
                               in_=s[r0:r0 + rr_, cb:cb + c]), "dp0")
                P.wait("sp", "dp0", ld)
                stc = P.dma("sp",
                            lambda e, d=dst, st=stage, r0=r, rr_=rr, cb=c0,
                            c=cc: e.dma_start(
                                out=d[r0:r0 + rr_, cb:cb + c],
                                in_=st[0:rr_, 0:c]), "ds0")
                stage_last_store[0] = stc
                cast_idx[0] += 1
        return None

    for (s_, d_, r_, c_) in [
        (ipw, ipw_bf, D, 2 * DI), (xpw, xpw_bf_d, DI, 96),
        (dtw, dtw_bf_d, DTR, DI), (opw, opw_bf, DI, D),
    ]:
        cast_done = cast_weight(s_, d_, r_, c_)
    ffn_cast_chunks = []
    for (s_, d_, r_, c_) in [
        (w1t, w1_bf, D, HF), (w2t, w2_bf, D, HF),
        (h1t, h1_bf, HF, HF), (h2t, h2_bf, HF, HF), (w3t, w3_bf, HF, D),
    ]:
        for r in range(0, r_, 128):
            rr = min(128, r_ - r)
            for c0 in range(0, c_, 2048):
                cc = min(2048, c_ - c0)
                ffn_cast_chunks.append((s_, d_, r, rr, c0, cc))

    def emit_ffn_casts(n):
        for _ in range(n):
            if not ffn_cast_chunks:
                return
            s_, d_, r0, rr, c0, cc = ffn_cast_chunks.pop(0)
            stage = STG0
            P.wait("pool", "ds0", stage_last_store[0])
            ld = P.dma("pool",
                       lambda e, s=s_, st=stage, rb=r0, rr_=rr, cb=c0, c=cc:
                       e.dma_start(out=st[0:rr_, 0:c],
                                   in_=s[rb:rb + rr_, cb:cb + c]), "dp0")
            P.wait("act", "dp0", ld)
            stc = P.dma("act",
                        lambda e, d=d_, st=stage, rb=r0, rr_=rr, cb=c0, c=cc:
                        e.dma_start(out=d[rb:rb + rr_, cb:cb + c],
                                    in_=st[0:rr_, 0:c]), "ds0")
            stage_last_store[0] = stc
            cast_idx[0] += 1

    P.wait("sp", "ds0", P.dma_cnt.get("ds0"))
    P.wait("sp", "ds1", P.dma_cnt.get("ds1"))
    g = alloc_mamba()
    WS, XPW, DTW = g["WS"], g["XPW"], g["DTW"]
    XSEG, H0B, XIN, SCR = g["XSEG"], g["H0B"], g["XIN"], g["SCR"]
    XZM, CONVACC, BCR = g["XZM"], g["CONVACC"], g["BCR"]
    XDBLP = (g["XDBL0"], g["XDBL1"])
    BIL, CIL = g["BIL"], g["CIL"]
    MBX, SBX, CBX = g["MBX"], g["SBX"], g["CBX"]
    FSCR, XMST, OPSL = g["FSCR"], g["XMST"], g["OPSL"]
    XMD, SLD = g["XMD"], g["SLD"]
    DBUIL, YBF = g["DBUIL"], g["YBF"]
    HIL, YPB = g["HIL"], g["YPB"]
    DELTA0, DELTA1, SPT = g["DELTA0"], g["DELTA1"], g["SPT"]
    DA0, DA1, YSSM = g["DA0"], g["DA1"], g["YSSM"]
    DU0, DU1 = g["DU0"], g["DU1"]
    R0 = OPSL[0:1, 0:TC]
    R1 = OPSL[0:1, TC:2 * TC]
    C0R = CONVACC[0:1, 0:TC]
    F0R = FSCR[0:1, 0:TC]
    P.dma("sp", lambda e: e.dma_start(
        out=XPW[:].rearrange("p (k n) -> p k n", n=96),
        in_=xpw_bf_d.rearrange("(k p) n -> p k n", p=128)), "dres")
    ld_res = P.dma("sp", lambda e: e.dma_start(out=DTW[:], in_=dtw_bf_d),
                   "dres")

    marks = {
        "bil_consumed_pool": None, "il_done": None, "selmm_done": None,
        "last_sp_act": None, "ws_pe_done": None,
        "oproj_evac_act": None, "oproj_evac_dve": None,
        "oproj_spill": {"dsp1": None, "dsp3": None},
        "h0b_done_pool": None, "stats_pe_done": None, "xin_done": None,
        "stats_evac_act": None, "inproj_evac_act": None,
        "x1_spill_total": None, "out2_spill_total": None,
        "scan_gate_done": None, "init_zero_dve": mk_zero,
        "dres_ld": ld_res,
    }

    def ckpt(tag):
        P.phase_marks.append((tag, {e: len(P.ops[e]) for e in P.ENGS}))

    def front_steps(gseg, pid, first_pass_reset=False):
        nw = N1W if pid == 0 else N2W
        t0 = gseg * TC
        src = xT if pid == 0 else x1_sp
        if first_pass_reset:
            P.wait("dve", "act", P.cnt["act"])
            P.op("dve", lambda v: v.memset(CCAR[:], 0.0))
            marks["ccar_reset"] = P.mark("dve")
        P.wait("sp", "pool", marks["h0b_done_pool"])
        P.wait("sp", "dve", marks["xin_done"])
        P.wait("sp", "pe", marks["stats_pe_done"])
        if pid == 1:
            P.wait("sp", "dsp1", marks.get(("x1_seg", gseg)))
        ldx = P.dma("sp", lambda e, tt=t0, s=src: e.dma_start(
            out=XSEG[:].rearrange("p (k n) -> p k n", n=TC),
            in_=s.rearrange("(k p) n -> p k n", p=128)[:, :, tt:tt + TC]),
            "dX")
        yield
        P.wait("act", "dX", ldx)
        P.wait("act", "pool", marks["h0b_done_pool"])
        for _s, _c in marks["oproj_spill"].items():
            P.wait("act", _s, _c)
            P.wait("dve", _s, _c)
        P.wait("pe", "act", marks["oproj_evac_act"])
        P.wait("pe", "dve", marks["oproj_evac_dve"])
        mmh = {}
        for k in range(DT_K):
            sl = OPSL[:, (k % 2) * TC:(k % 2) * TC + TC]
            P.wait("act", "pe", mmh.get(k - 2))
            ACT(sl, XSEG[:, k * TC:(k + 1) * TC], AF.Square)
            mk_a = P.mark("act")
            P.wait("pe", "act", mk_a)
            if pid == 0:
                MM(ps[0][0:1, 0:TC], ONESK[:], XSEG[:, k * TC:(k + 1) * TC],
                   k == 0, k == DT_K - 1)
            MM(ps[1][0:1, 0:TC], ONESK[:], sl, k == 0, k == DT_K - 1)
            mmh[k] = P.mark("pe")
            if k == 3:
                yield
        marks["stats_pe_done"] = mmh[DT_K - 1]
        P.wait("act", "pe", mmh[DT_K - 1])
        if pid == 0:
            ACT(R0, ps[0][0:1, 0:TC], AF.Copy, scale=1.0 / D)
            ACT(R1, ps[1][0:1, 0:TC], AF.Copy, scale=1.0 / D)
            ACT(C0R, R0, AF.Square)
            mk_a = P.mark("act")
            P.wait("dve", "act", mk_a)
            P.wait("dve", "pool", marks["h0b_done_pool"])
            TT(R1, R1, C0R, ALU.subtract)
            mk_d = P.mark("dve")
            P.wait("act", "dve", mk_d)
            ACT(C0R, R1, AF.Ln, bias=EPS)
            ACT(F0R, C0R, AF.Exp, scale=-1.0)
            mk_a = P.mark("act")
            P.wait("dve", "act", mk_a)
            TT(F0R, R1, F0R, ALU.mult)
            mk_d = P.mark("dve")
            P.wait("act", "dve", mk_d)
            ACT(R1, C0R, AF.Exp, scale=-0.5)
            ACT(C0R, F0R, AF.Ln, bias=EPS)
            ACT(F0R, C0R, AF.Exp, scale=-0.5)
            mk_a = P.mark("act")
            P.wait("dve", "act", mk_a)
            TT(C0R, R1, F0R, ALU.mult)
            mk_d = P.mark("dve")
            P.wait("pe", "act", mk_a)
            P.wait("pe", "dve", mk_d)
            MM(ps[0][:, 0:TC], ONES1[:], R0, True, True)
            MM(ps[1][:, 0:TC], ONES1[:], R1, True, True)
            MM(ps[2][:, 0:TC], ONES1[:], C0R, True, True)
            mk_b = P.mark("pe")
            P.wait("act", "pe", mk_b)
            ACT(MBX[:, 0:TC], ps[0][:, 0:TC], AF.Copy)
            ACT(SBX[:, 0:TC], ps[1][:, 0:TC], AF.Copy)
            ACT(CBX[:, 0:TC], ps[2][:, 0:TC], AF.Copy)
            mk_bc = P.mark("act")
            marks["stats_evac_act"] = mk_bc
            yield
            P.wait("dve", "act", mk_bc)
            for k in range(DT_K):
                TT(FSCR[:, 0:TC], XSEG[:, k * TC:(k + 1) * TC], MBX[:, 0:TC],
                   ALU.subtract)
                STT(XIN[:, k * TC:(k + 1) * TC], FSCR[:, 0:TC], nw[:, k:k + 1],
                    CBX[:, 0:TC], ALU.mult, ALU.mult)
            mk_xin = P.mark("dve")
        else:
            ACT(R1, ps[1][0:1, 0:TC], AF.Copy, scale=1.0 / D)
            ACT(R1, R1, AF.Ln, bias=EPS)
            ACT(R1, R1, AF.Exp, scale=-0.5)
            mk_a = P.mark("act")
            P.wait("pe", "act", mk_a)
            MM(ps[1][:, 0:TC], ONES1[:], R1, True, True)
            mk_b = P.mark("pe")
            P.wait("act", "pe", mk_b)
            ACT(SBX[:, 0:TC], ps[1][:, 0:TC], AF.Copy)
            mk_bc = P.mark("act")
            marks["stats_evac_act"] = mk_bc
            yield
            P.wait("dve", "act", mk_bc)
            for k in range(DT_K):
                STT(XIN[:, k * TC:(k + 1) * TC], XSEG[:, k * TC:(k + 1) * TC],
                    nw[:, k:k + 1], SBX[:, 0:TC], ALU.mult, ALU.mult)
            mk_xin = P.mark("dve")
        marks["xin_done"] = mk_xin
        yield
        P.wait("pe", "dve", mk_xin)
        P.wait("pe", "dres", marks["dres_ld"])
        if pid == 0 and gseg == 0:
            P.wait("act", "dve", marks["init_zero_dve"])
        mk_e8 = {}
        slot_guard = {0: [], 1: []}
        mk_xdbl_mm = None

        def ld_e8(e8):
            half = e8 % 2
            P.wait("sp", "pe", mk_e8.get(e8 - 2, marks["ws_pe_done"]))
            return P.dma("sp", lambda e, ee=e8, hh=half: e.dma_start(
                out=WS[:, hh * 4096:(hh + 1) * 4096].rearrange(
                    "p (k n) -> p k n", n=512),
                in_=ipw_bf.rearrange("(k p) n -> p k n", p=128)
                [:, :, ee * 512:(ee + 1) * 512]), "dW")

        lde = {0: ld_e8(0)}
        for e8 in range(8):
            if e8 + 1 < 8:
                lde[e8 + 1] = ld_e8(e8 + 1)
            half = e8 % 2
            P.wait("pe", "dW", lde[e8])
            for mi in range(4):
                m = e8 * 4 + mi
                i = m if m < 16 else m - 16
                pb = ps[2 + (mi % 2)]
                for k in range(DT_K):
                    MM(pb[:, 0:TC],
                       WS[:, half * 4096 + k * 512 + mi * 128:
                          half * 4096 + k * 512 + mi * 128 + 128],
                       XIN[:].rearrange("p (k n) -> p k n", n=TC)[:, k, :],
                       k == 0, k == DT_K - 1)
                mk_mm = P.mark("pe")
                mk_e8[e8] = mk_mm
                P.wait("act", "pe", mk_mm)
                if m < 16:
                    if first_pass_reset and i == 0:
                        P.wait("act", "dve", marks["ccar_reset"])
                    ACT(XZM[:, 3:TC + 3], pb[:, 0:TC], AF.Identity,
                        bias=IPB[:, i:i + 1])
                    ACT(XZM[:, 0:3], CCAR[:, i * 3:i * 3 + 3], AF.Copy)
                    ACT(CONVACC[:, 0:TC], XZM[:, 0:TC], AF.Identity,
                        scale=CONVW[:, 4 * i:4 * i + 1])
                    mk_c = P.mark("act")
                    P.wait("dve", "act", mk_c)
                    for j in range(1, 4):
                        STT(CONVACC[:, 0:TC], XZM[:, j:TC + j],
                            CONVW[:, 4 * i + j:4 * i + j + 1],
                            CONVACC[:, 0:TC], ALU.mult, ALU.add)
                    mk_c2 = P.mark("dve")
                    P.wait("act", "dve", mk_c2)
                    for _s, _c in slot_guard[i % 2]:
                        P.wait("act", _s, _c)
                    slot = XMST[:, (i % 2) * TC:(i % 2) * TC + TC]
                    ACT(slot, CONVACC[:, 0:TC], AF.Silu,
                        bias=CONVB[:, i:i + 1])
                    ACT(CCAR[:, i * 3:i * 3 + 3], XZM[:, TC:TC + 3], AF.Copy)
                    mk_xm = P.mark("act")
                    P.wait("pe", "act", mk_xm)
                    MM(ps[4][0:96, 0:TC],
                       XPW[:].rearrange("p (k n) -> p k n", n=96)[:, i, :],
                       slot, i == 0, i == 15)
                    mk_xdbl_mm = P.mark("pe")
                    P.wait("sp", "act", mk_xm)
                    sp_mk = P.dma(
                        "sp", lambda e, ii=i, tt=t0, sl=slot: e.dma_start(
                            out=xm_sp[ii * 128:(ii + 1) * 128, tt:tt + TC],
                            in_=sl), "dxm")
                    slot_guard[i % 2] = [("dxm", sp_mk), ("pe", mk_xdbl_mm)]
                else:
                    for _s, _c in slot_guard[i % 2]:
                        P.wait("act", _s, _c)
                    slot = XMST[:, (i % 2) * TC:(i % 2) * TC + TC]
                    ACT(slot, pb[:, 0:TC], AF.Silu,
                        bias=IPB[:, 16 + i:16 + i + 1])
                    mk_s = P.mark("act")
                    P.wait("pe", "act", mk_s)
                    P.wait("sp", "act", mk_s)
                    sp_mk = P.dma(
                        "sp", lambda e, ii=i, tt=t0, sl=slot: e.dma_start(
                            out=sil_sp[ii * 128:(ii + 1) * 128, tt:tt + TC],
                            in_=sl), "dsl")
                    slot_guard[i % 2] = [("dsl", sp_mk)]
                yield
        marks["ws_pe_done"] = P.cnt["pe"]
        marks["inproj_evac_act"] = P.cnt["act"]
        marks[("xm_spilled", gseg)] = P.dma_cnt.get("dxm")
        marks[("sil_spilled", gseg)] = P.dma_cnt.get("dsl")
        marks[("xdbl_mm", gseg)] = mk_xdbl_mm

    def finish_front(gseg, pid):
        XDBL = XDBLP[gseg % 2]
        P.wait("act", "pe", marks[("xdbl_mm", gseg)])
        P.wait("act", "pe", marks["selmm_done"])
        P.wait("act", "pool", marks["bil_consumed_pool"])
        ACT(XDBL[:, 0:TC], ps[4][0:64, 0:TC], AF.Copy)
        ACT(BCR[:, 0:TC], ps[4][64:96, 0:TC], AF.Copy)
        mk_bcr = P.mark("act")
        marks[("xdbl_evac", gseg)] = mk_bcr
        P.wait("pe", "act", mk_bcr)
        P.wait("pe", "act", marks["last_sp_act"])
        ev_c = None
        for s in range(16):
            MM(ps[5][:, 0:TC], SELB[:, s * 128:(s + 1) * 128], BCR[:, 0:TC],
               True, True)
            mm_b = P.mark("pe")
            P.wait("act", "pe", mm_b)
            ACT(BIL[:].rearrange("p (t s) -> p t s", s=16)[:, :, s],
                ps[5][:, 0:TC], AF.Copy)
            ev_b = P.mark("act")
            MM(ps[6][:, 0:TC], SELC[:, s * 128:(s + 1) * 128], BCR[:, 0:TC],
               True, True)
            mm_c = P.mark("pe")
            P.wait("act", "pe", mm_c)
            ACT(CIL[:].rearrange("p (t s) -> p t s", s=16)[:, :, s],
                ps[6][:, 0:TC], AF.Copy)
            ev_c = P.mark("act")
            P.wait("pe", "act", ev_b)
            P.wait("pe", "act", ev_c)
        marks["selmm_done"] = P.cnt["pe"]
        marks["il_done"] = ev_c

    def emit_scan(gseg, pid, fit):
        t0 = gseg * TC
        XDBL = XDBLP[gseg % 2]
        mk_bcr = marks[("xdbl_evac", gseg)]
        oproj_guard = marks["ws_pe_done"]

        def pump(n):
            if fit is None:
                return
            for _ in range(n):
                if next(fit, "DONE") == "DONE":
                    break

        if pid == 1 and gseg == 0:
            P.wait("dve", "act", P.cnt["act"])
            P.op("dve", lambda v: v.memset(SCAR[:], 0.0))
            mk_rst = P.mark("dve")
            P.wait("act", "dve", mk_rst)
        if pid == 0:
            P.wait("pool", "dve", marks["xin_done"])
            P.wait("pool", "dve", marks["oproj_evac_dve"])
            P.wait("pool", "act", marks["stats_evac_act"])
            P.wait("pool", "pe", marks[("xdbl_mm", gseg)])
            P.wait("pool", "dxm", marks[("xm_spilled", gseg)])
            P.wait("pool", "dsl", marks[("sil_spilled", gseg)])
            for k in range(DT_K):
                P.op("pool", lambda p_, kk=k: p_.tensor_tensor(
                    FSCR[:, 0:TC], XSEG[:, kk * TC:(kk + 1) * TC],
                    MBX[:, 0:TC], ALU.subtract))
                P.op("pool", lambda p_, kk=k: p_.tensor_tensor(
                    H0B[:, kk * TC:(kk + 1) * TC], FSCR[:, 0:TC],
                    SBX[:, 0:TC], ALU.mult))
            marks["h0b_done_pool"] = P.mark("pool")
        P.wait("pool", "act", marks["il_done"])
        P.wait("sp", "dxm", marks[("xm_spilled", gseg)])
        P.wait("sp", "dsl", marks[("sil_spilled", gseg)])
        hist = marks.setdefault("hist", {})

        def emit_ld(i):
            P.wait("sp", "dve", marks.get(("gate_par", i % 2)))
            p_ = i % 2
            marks[("ldm", i)] = P.dma(
                "sp", lambda e, ii=i, tt=t0, pp=p_: e.dma_start(
                    out=XMD[:, pp * TC:(pp + 1) * TC],
                    in_=xm_sp[ii * 128:(ii + 1) * 128, tt:tt + TC]), "dxt")
            marks[("lds", i)] = P.dma(
                "sp", lambda e, ii=i, tt=t0, pp=p_: e.dma_start(
                    out=SLD[:, pp * TC:(pp + 1) * TC],
                    in_=sil_sp[ii * 128:(ii + 1) * 128, tt:tt + TC]), "dst")

        def emit_dt_sp(i):
            dl = DELTA0 if i % 2 == 0 else DELTA1
            P.wait("pe", "act", mk_bcr)
            P.wait("pe", "act", marks["last_sp_act"])
            MM(ps[5][:, 0:TC], DTW[:, i * 128:(i + 1) * 128], XDBL[:, 0:TC],
               True, True)
            mk = P.mark("pe")
            P.wait("act", "pe", mk)
            P.wait("act", "dve", marks.get(("dadd_mk", i - 1)))
            ACT(dl, ps[5][:, 0:TC], AF.Relu, bias=DTB[:, i:i + 1])
            ACT(SPT, ps[5][:, 0:TC], AF.Abs, bias=DTB[:, i:i + 1])
            ACT(SPT, SPT, AF.Exp, scale=-1.0)
            ACT(SPT, SPT, AF.Ln, bias=1.0)
            mk_sp = P.mark("act")
            marks["last_sp_act"] = mk_sp
            return mk_sp

        def emit_dadd_du(i, mk_sp):
            dl = DELTA0 if i % 2 == 0 else DELTA1
            du = DU0 if i % 2 == 0 else DU1
            P.wait("dve", "act", mk_sp)
            TT(dl, dl, SPT, ALU.add)
            marks[("dadd_mk", i)] = P.mark("dve")
            P.wait("dve", "dxt", marks.get(("ldm", i)))
            P.wait("dve", "pool", marks.get(("dbu_done", i - 2)))
            TT(du, dl, XMD[:, (i % 2) * TC:(i % 2 + 1) * TC], ALU.mult)
            return P.mark("dve")

        def emit_dbu_half(i, mk_du, h):
            du = DU0 if i % 2 == 0 else DU1
            duv = du.unsqueeze(2).broadcast_to([128, TC, 16])
            P.wait("pool", "dve", mk_du)
            P.wait("pool", "dve", hist.get(("war", h)))
            P.op("pool", lambda p_, dv=duv, hh=h: p_.tensor_tensor(
                DBUIL[:].rearrange("p (t s) -> p t s", s=16)
                [:, :, hh * 8:(hh + 1) * 8],
                dv[:, :, hh * 8:(hh + 1) * 8],
                BIL[:].rearrange("p (t s) -> p t s", s=16)
                [:, :, hh * 8:(hh + 1) * 8], ALU.mult))
            hist[("dbu", h)] = P.mark("pool")
            if h == 1:
                marks[("dbu_done", i)] = hist[("dbu", 1)]

        emit_ld(0)
        emit_ld(1)
        mk_spc = emit_dt_sp(0)
        mk_duc = emit_dadd_du(0, mk_spc)
        emit_dbu_half(0, mk_duc, 0)
        emit_dbu_half(0, mk_duc, 1)
        mk_y = None
        for i in range(16):
            if i + 1 < 16:
                mk_spn = emit_dt_sp(i + 1)
                mk_dun = emit_dadd_du(i + 1, mk_spn)
            dl = DELTA0 if i % 2 == 0 else DELTA1
            P.wait("act", "dve", marks.get(("dadd_mk", i)))
            for s in range(16):
                dab = DA0 if s % 2 == 0 else DA1
                P.wait("act", "dve", hist.get((s - 2) % 16))
                ACT(dab, dl, AF.Exp,
                    scale=ASB[:, 16 * i + s:16 * i + s + 1])
                mk_da = P.mark("act")
                P.wait("dve", "act", mk_da)
                P.wait("dve", "pool", hist[("dbu", s // 8)])
                P.op("dve", lambda v, ss=s, db=dab, ii=i:
                     v.tensor_tensor_scan(
                         HIL[:].rearrange("p (t s) -> p t s", s=16)
                         [:, :, ss],
                         db,
                         DBUIL[:].rearrange("p (t s) -> p t s", s=16)
                         [:, :, ss],
                         SCAR[:, 16 * ii + ss:16 * ii + ss + 1],
                         ALU.mult, ALU.add))
                mk_scan = P.mark("dve")
                hist[s] = mk_scan
                if s == 7:
                    hist[("war", 0)] = mk_scan
                elif s == 15:
                    hist[("war", 1)] = mk_scan
            if i + 1 < 16:
                emit_dbu_half(i + 1, mk_dun, 0)
            P.wait("act", "dve", hist[15])
            ACT(SCAR[:, 16 * i:16 * i + 16],
                HIL[:, (TC - 1) * 16:(TC - 1) * 16 + 16], AF.Copy)
            P.mark("act")
            ym_mk = {}
            for th in range(4):
                P.wait("pool", "dve", hist[15])
                P.op("pool", lambda p_, tt=th: p_.tensor_tensor(
                    YPB[:].rearrange("p (t s) -> p t s", s=16)
                    [:, tt * 128:(tt + 1) * 128, :],
                    HIL[:].rearrange("p (t s) -> p t s", s=16)
                    [:, tt * 128:(tt + 1) * 128, :],
                    CIL[:].rearrange("p (t s) -> p t s", s=16)
                    [:, tt * 128:(tt + 1) * 128, :], ALU.mult))
                ym_mk[th] = P.mark("pool")
            if i + 1 < 16:
                emit_dbu_half(i + 1, mk_dun, 1)
            pump(3)
            for th in range(4):
                P.wait("dve", "pool", ym_mk[th])
                P.op("dve", lambda v, tt=th: v.tensor_reduce(
                    YSSM[:, tt * 128:(tt + 1) * 128],
                    YPB[:].rearrange("p (t s) -> p t s", s=16)
                    [:, tt * 128:(tt + 1) * 128, :],
                    axis=AX.X, op=ALU.add))
            P.wait("dve", "dst", marks.get(("lds", i)))
            par = i % 2
            STT(YSSM, XMD[:, par * TC:(par + 1) * TC], DSSM[:, i:i + 1],
                YSSM, ALU.mult, ALU.add)
            if i == 0:
                P.wait("dve", "pe", oproj_guard)
            TT(YBF[:, i * TC:(i + 1) * TC], YSSM,
               SLD[:, par * TC:(par + 1) * TC], ALU.mult)
            mk_y = P.mark("dve")
            marks[("gate_par", i % 2)] = mk_y
            if i + 2 < 16:
                emit_ld(i + 2)
        marks["bil_consumed_pool"] = P.cnt["pool"]
        marks["scan_gate_done"] = mk_y
        pump(100)

    def emit_oproj(gseg, pid):
        t0 = gseg * TC
        opw_v = opw_bf.rearrange("(k p) n -> p k n", p=128)
        spill_sem = "dsp1" if pid == 0 else "dsp3"
        dst_sp = x1_sp if pid == 0 else out2_sp
        slot_spill = {0: None, 1: None}
        bank_evac = {}
        ev_act, ev_dve = None, None
        P.wait("pe", "dve", marks["scan_gate_done"])
        P.wait("pe", "act", marks["inproj_evac_act"])
        P.wait("pe", "act", marks["stats_evac_act"])
        for half in range(2):
            P.wait("sp", "pe", marks["ws_pe_done"])
            ldop = P.dma("sp", lambda e, hh=half: e.dma_start(
                out=WS[:].rearrange("p (k n) -> p k n", n=512),
                in_=opw_v[:, :, hh * 512:(hh + 1) * 512]), "dW")
            P.wait("pe", "dW", ldop)
            for m4 in range(4):
                m = half * 4 + m4
                pb = ps[m4]
                if m4 in bank_evac:
                    P.wait("pe", bank_evac[m4][0], bank_evac[m4][1])
                for k8 in range(16):
                    MM(pb[:, 0:TC],
                       WS[:, k8 * 512 + m4 * 128:k8 * 512 + m4 * 128 + 128],
                       YBF[:, k8 * TC:(k8 + 1) * TC], k8 == 0, k8 == 15)
                mk_mm = P.mark("pe")
                slot = OPSL[:, (m % 2) * TC:(m % 2 + 1) * TC]
                if pid == 0:
                    P.wait("dve", "pe", mk_mm)
                    P.wait("dve", "pool", marks["h0b_done_pool"])
                    P.wait("dve", spill_sem, slot_spill[m % 2])
                    STT(slot, pb[:, 0:TC], OPB[:, m:m + 1],
                        H0B[:, m * TC:(m + 1) * TC], ALU.add, ALU.add)
                    ev_dve = P.mark("dve")
                    bank_evac[m4] = ("dve", ev_dve)
                    P.wait("sp", "dve", ev_dve)
                else:
                    P.wait("act", "pe", mk_mm)
                    P.wait("act", spill_sem, slot_spill[m % 2])
                    ACT(slot, pb[:, 0:TC], AF.Identity, bias=OPB[:, m:m + 1])
                    ev_act = P.mark("act")
                    bank_evac[m4] = ("act", ev_act)
                    P.wait("sp", "act", ev_act)
                slot_spill[m % 2] = P.dma(
                    "sp", lambda e, mm=m, tt=t0, sl=slot: e.dma_start(
                        out=dst_sp[mm * 128:(mm + 1) * 128, tt:tt + TC],
                        in_=sl), spill_sem)
            marks["ws_pe_done"] = P.cnt["pe"]
        marks["oproj_spill"][spill_sem] = slot_spill[1]
        if ev_act is not None:
            marks["oproj_evac_act"] = ev_act
        if ev_dve is not None:
            marks["oproj_evac_dve"] = ev_dve
        if pid == 0:
            marks[("x1_seg", gseg)] = P.dma_cnt["dsp1"]
            marks["x1_spill_total"] = P.dma_cnt["dsp1"]
        else:
            marks["out2_spill_total"] = P.dma_cnt["dsp3"]

    units = [(0, gg) for gg in range(NSEG)]
    if stop_after not in ("p1", "pro"):
        units += [(1, gg) for gg in range(NSEG)]
    if stop_after != "pro":
        fit0 = front_steps(0, 0)
        for _ in range(200):
            if next(fit0, "DONE") == "DONE":
                break
        finish_front(0, 0)
        for ui, (pid, gg) in enumerate(units):
            ckpt(f"m{pid}s{gg}_scan")
            emit_ffn_casts(12)
            nxt = units[ui + 1] if ui + 1 < len(units) else None
            fit = None
            if nxt is not None:
                fit = front_steps(
                    nxt[1], nxt[0],
                    first_pass_reset=(nxt[0] == 1 and nxt[1] == 0))
            emit_scan(gg, pid, fit)
            if nxt is not None:
                finish_front(nxt[1], nxt[0])
            ckpt(f"m{pid}s{gg}_oproj")
            emit_oproj(gg, pid)
    st_m.close()

    run_2b = stop_after in (None, "p2b", "p3")
    st_2b = _ctx.ExitStack()
    bar_pe, bar_act = P.cnt["pe"], P.cnt["act"]
    bar_dve, bar_pool = P.cnt["dve"], P.cnt["pool"]
    for w in ("sp", "dve"):
        for eng, cnt in (("pe", bar_pe), ("act", bar_act), ("dve", bar_dve),
                         ("pool", bar_pool)):
            if w != eng and cnt:
                P.wait(w, eng, cnt)
        for dsem in ("dsp1", "dsp2", "dsp3", "dxm", "dsl", "dxt", "dst",
                     "dW", "dX"):
            P.wait(w, dsem, P.dma_cnt.get(dsem))
    WWW = mksb(st_2b, "WWW", [128, DT_K * 1024], F32)
    WRW = mksb(st_2b, "WRW", [128, DT_K * 1024], F32)
    WT = mksb(st_2b, "WT", [128, DT_K * 1024], F32)
    O2SEG = mksb(st_2b, "O2SEG", [128, DT_K * (TC + 1)], F32)
    VTM = mksb(st_2b, "VTM", [128, DT_K * 128], F32)
    VGTM = mksb(st_2b, "VGTM", [128, DT_K * 128], F32)
    WKTM = mksb(st_2b, "WKTM", [128, DT_K * 128], F32)
    SMT = mksb(st_2b, "SMT", [128, 128], F32)
    RKG = mksb(st_2b, "RKG", [128, DT_K * 128], F32)
    RDB = mksb(st_2b, "RDB", [128, DT_K * TC], F32)
    ACC2 = mksb(st_2b, "ACC2", [128, DT_K * TC], F32)
    X2SEG = mksb(st_2b, "X2SEG", [128, DT_K * TC], F32)
    P.op("dve", lambda v: v.memset(WT[:], 0.0))
    for k in range(DT_K):
        P.op("dve", lambda v, kk=k: v.memset(
            O2SEG[:, kk * (TC + 1):kk * (TC + 1) + 1], 0.0))
    mk_zero = P.mark("dve")
    P.dma("sp", lambda e: e.dma_start(
        out=WWW[:].rearrange("p (k n) -> p k n", n=1024),
        in_=wwt.rearrange("(k p) n -> p k n", p=128)), "dwr")
    ldw = P.dma("sp", lambda e: e.dma_start(
        out=WRW[:].rearrange("p (k n) -> p k n", n=1024),
        in_=wrt.rearrange("(k p) n -> p k n", p=128)), "dwr")
    P.wait("pe", "dwr", ldw)
    P.wait("dve", "act", mk_ginr)

    mk_lead = mk_zero
    mk_xa_dve = None
    for g in range(NSEG if run_2b else 0):
        ckpt(f"2b_s{g}")
        t0 = g * TC
        P.wait("sp", "dve", mk_lead)
        P.wait("sp", "dve", mk_xa_dve)
        P.wait("sp", "dsp3", marks.get("out2_spill_total"))
        ldo = P.dma("sp", lambda e, tt=t0: e.dma_start(
            out=O2SEG[:].rearrange("p (k n) -> p k n", n=TC + 1)[:, :, 1:TC + 1],
            in_=out2_sp.rearrange("(k p) n -> p k n", p=128)[:, :, tt:tt + TC]),
            "dO")
        P.wait("sp", "dsp1", marks.get("x1_spill_total"))
        ldacc = P.dma("sp", lambda e, tt=t0: e.dma_start(
            out=ACC2[:].rearrange("p (k n) -> p k n", n=TC),
            in_=x1_sp.rearrange("(k p) n -> p k n", p=128)[:, :, tt:tt + TC]),
            "dA2")
        ldxx = P.dma("sp", lambda e, tt=t0: e.dma_start(
            out=X2SEG[:].rearrange("p (k n) -> p k n", n=TC),
            in_=xT.rearrange("(k p) n -> p k n", p=128)[:, :, tt:tt + TC]),
            "dX2")
        P.wait("pe", "dO", ldo)

        def o2(k, c0, c1):
            return O2SEG[:, k * (TC + 1) + c0:k * (TC + 1) + c1]

        for c in range(NCH):
            cc = c * CH
            for nh in range(2):
                pb = ps[0 + nh]
                for k in range(DT_K):
                    MM(pb[:, 0:512], o2(k, cc + 1, cc + CH + 1),
                       WWW[:, k * 1024 + nh * 512:k * 1024 + (nh + 1) * 512],
                       k == 0, k == DT_K - 1)
            mkv = P.mark("pe")
            P.wait("act", "pe", mkv)
            for nh in range(2):
                ACT(VTM[:, nh * 512:(nh + 1) * 512], ps[nh][:, 0:512], AF.Copy)
                ACT(VGTM[:, nh * 512:(nh + 1) * 512], ps[nh][:, 0:512],
                    AF.Identity, scale=GW[:, 0:1])
            mk_v = P.mark("act")
            P.wait("pe", "act", mk_v)
            mk_te = None
            for k in range(DT_K):
                P.op("pe", lambda t, kk=k, c0=cc: t.transpose(
                    ps[2][:, (kk % 2) * 128:(kk % 2) * 128 + 128],
                    o2(kk, c0, c0 + CH), EYE[:]))
                mkT = P.mark("pe")
                P.wait("act", "pe", mkT)
                ACT(WKTM[:, k * 128:(k + 1) * 128],
                    ps[2][:, (k % 2) * 128:(k % 2) * 128 + 128], AF.Copy)
                mk_te = P.mark("act")
                P.wait("pe", "act", mk_te)
            for k in range(DT_K):
                MM(ps[3][:, 0:128], o2(k, cc, cc + CH), o2(k, cc + 1, cc + CH + 1),
                   k == 0, k == DT_K - 1)
            mk_st = P.mark("pe")
            P.wait("dve", "pe", mk_st)
            TT(SMT[:], ps[3][:, 0:128], MT[:], ALU.mult)
            for k in range(DT_K):
                TT(RKG[:, k * 128:(k + 1) * 128], o2(k, cc + 1, cc + CH + 1),
                   GINR[:], ALU.mult)
            mk_rkg = P.mark("dve")
            P.wait("pe", "dve", mk_rkg)
            for m in range(DT_K):
                pb = ps[4 + (m % 2)]
                col = ((m // 2) % 2) * 128
                for k in range(DT_K):
                    MM(pb[:, col:col + 128],
                       WT[:, k * 1024 + m * 128:k * 1024 + m * 128 + 128],
                       RKG[:, k * 128:(k + 1) * 128], k == 0, False)
                MM(pb[:, col:col + 128], VTM[:, m * 128:(m + 1) * 128], SMT[:],
                   False, True)
                mk_rd = P.mark("pe")
                P.wait("act", "pe", mk_rd)
                ACT(RDB[:, m * TC + cc:m * TC + cc + CH], pb[:, col:col + 128],
                    AF.Copy)
                mk_rde = P.mark("act")
                P.wait("pe", "act", mk_rde)
            mk_wb_hist = {}
            for m in range(DT_K):
                for nh in range(2):
                    j = m * 2 + nh
                    pb = ps[6] if j % 2 == 0 else ps[3]
                    P.wait("pe", "dve", mk_wb_hist.get(j - 2))
                    MM(pb[:, 0:512], WKTM[:, m * 128:(m + 1) * 128],
                       VGTM[:, nh * 512:(nh + 1) * 512], True, True)
                    mk_wu = P.mark("pe")
                    P.wait("dve", "pe", mk_wu)
                    STT(WT[:, m * 1024 + nh * 512:m * 1024 + (nh + 1) * 512],
                        WT[:, m * 1024 + nh * 512:m * 1024 + (nh + 1) * 512],
                        GC[:, 0:1], pb[:, 0:512], ALU.mult, ALU.add)
                    mk_wb_hist[j] = P.mark("dve")
        P.op("dve", lambda v: v.tensor_copy(
            O2SEG[:].rearrange("p (k n) -> p k n", n=TC + 1)[:, :, 0:1],
            O2SEG[:].rearrange("p (k n) -> p k n", n=TC + 1)[:, :, TC:TC + 1]))
        mk_lead = P.mark("dve")
        P.wait("pe", "dve", mk_lead)
        mk_xa_l = []
        for m in range(DT_K):
            pb = ps[m % 4]
            if m >= 4:
                P.wait("pe", "dve", mk_xa_l[m - 4])
            for k in range(DT_K):
                MM(pb[:, 0:TC],
                   WRW[:, k * 1024 + m * 128:k * 1024 + m * 128 + 128],
                   RDB[:, k * TC:(k + 1) * TC], k == 0, k == DT_K - 1)
            mk_rp = P.mark("pe")
            P.wait("dve", "pe", mk_rp)
            P.wait("dve", "dA2", ldacc)
            P.wait("dve", "dX2", ldxx)
            STT(ACC2[:, m * TC:(m + 1) * TC], pb[:, 0:TC], ALPHA,
                ACC2[:, m * TC:(m + 1) * TC], ALU.mult, ALU.add)
            TT(ACC2[:, m * TC:(m + 1) * TC], ACC2[:, m * TC:(m + 1) * TC],
               X2SEG[:, m * TC:(m + 1) * TC], ALU.add)
            TT(ACC2[:, m * TC:(m + 1) * TC], ACC2[:, m * TC:(m + 1) * TC],
               O2SEG[:, m * (TC + 1) + 1:m * (TC + 1) + TC + 1], ALU.add)
            mk_xa_l.append(P.mark("dve"))
        mk_xa_dve = mk_xa_l[-1]
        P.wait("sp", "dve", mk_xa_dve)
        marks["xa_spill_total"] = P.dma(
            "sp", lambda e, tt=t0: e.dma_start(
                out=xA_sp.rearrange("(k p) n -> p k n", p=128)
                [:, :, tt:tt + TC],
                in_=ACC2[:].rearrange("p (k n) -> p k n", n=TC)), "dxa")

    st_2b.close()
    emit_ffn_casts(len(ffn_cast_chunks))
    P.wait("sp", "ds0", P.dma_cnt.get("ds0"))
    P.wait("sp", "ds1", P.dma_cnt.get("ds1"))
    st_pro.close()
    run_3 = stop_after in (None, "p3")
    bar_pe, bar_act = P.cnt["pe"], P.cnt["act"]
    bar_dve, bar_pool = P.cnt["dve"], P.cnt["pool"]
    for w in ("sp", "act", "dve"):
        for eng, cnt in (("pe", bar_pe), ("act", bar_act), ("dve", bar_dve),
                         ("pool", bar_pool)):
            if w != eng and cnt:
                P.wait(w, eng, cnt)
        for dsem in ("dO", "dA2", "dX2", "dxa", "dwr"):
            P.wait(w, dsem, P.dma_cnt.get(dsem))
    st_3 = _ctx.ExitStack()
    GBUF = mksb(st_3, "GBUF", [128, HF_K * TB], BF16)
    HOUT = mksb(st_3, "HOUT", [128, HF_K * TB], BF16)
    XAB = mksb(st_3, "XAB", [128, DT_K * TB], F32)
    XHB = mksb(st_3, "XHB", [128, DT_K * TB], BF16)
    WSL = mksb(st_3, "WSL", [128, HF_K * 128], BF16)
    _wf = WSL[:].bitcast(F32)
    MBF = _wf[:, 0:TB]
    SBF = _wf[:, TB:2 * TB]
    _gf = GBUF[:].bitcast(F32)
    MRW = _gf[0:1, 0:TB]
    VRW = _gf[0:1, TB:2 * TB]
    WSL2 = HOUT[:, 4 * TB:5 * TB]
    out_done = None
    sq = HOUT[:, 0:4 * TB].bitcast(F32)
    mk_p3_dve = mk_xa_dve
    for b in range(NTB if run_3 else 0):
        ckpt(f"ffn_b{b}")
        t0 = b * TB
        P.wait("sp", "dve", mk_p3_dve)
        P.wait("sp", "dxa", marks.get("xa_spill_total"))
        ldxa = P.dma("sp", lambda e, tt=t0: e.dma_start(
            out=XAB[:].rearrange("p (k n) -> p k n", n=TB),
            in_=xA_sp.rearrange("(k p) n -> p k n", p=128)[:, :, tt:tt + TB]),
            "dX3")
        P.wait("act", "dX3", ldxa)
        P.wait("pe", "dX3", ldxa)
        for k in range(DT_K):
            ACT(sq[:, 0:TB], XAB[:, k * TB:(k + 1) * TB], AF.Square)
            mk_sq = P.mark("act")
            P.wait("pe", "act", mk_sq)
            MM(ps[0][0:1, 0:512], ONESK[:], XAB[:, k * TB:k * TB + 512],
               k == 0, k == DT_K - 1)
            MM(ps[1][0:1, 0:512], ONESK[:], XAB[:, k * TB + 512:(k + 1) * TB],
               k == 0, k == DT_K - 1)
            MM(ps[2][0:1, 0:512], ONESK[:], sq[:, 0:512], k == 0,
               k == DT_K - 1)
            MM(ps[3][0:1, 0:512], ONESK[:], sq[:, 512:TB], k == 0,
               k == DT_K - 1)
            mk_smm = P.mark("pe")
            P.wait("act", "pe", mk_smm)
        mk_sums = P.mark("pe") if False else mk_smm
        P.wait("act", "pe", mk_sums)
        ACT(MRW[0:1, 0:512], ps[0][0:1, 0:512], AF.Copy, scale=1.0 / D)
        ACT(MRW[0:1, 512:TB], ps[1][0:1, 0:512], AF.Copy, scale=1.0 / D)
        ACT(VRW[0:1, 0:512], ps[2][0:1, 0:512], AF.Copy, scale=1.0 / D)
        ACT(VRW[0:1, 512:TB], ps[3][0:1, 0:512], AF.Copy, scale=1.0 / D)
        ACT(sq[0:1, 0:TB], MRW[0:1, 0:TB], AF.Square)
        mk_v = P.mark("act")
        P.wait("dve", "act", mk_v)
        TT(VRW[0:1, 0:TB], VRW[0:1, 0:TB], sq[0:1, 0:TB], ALU.subtract)
        mk_v2 = P.mark("dve")
        P.wait("act", "dve", mk_v2)
        ACT(VRW[0:1, 0:TB], VRW[0:1, 0:TB], AF.Ln, bias=EPS)
        ACT(VRW[0:1, 0:TB], VRW[0:1, 0:TB], AF.Exp, scale=-0.5)
        mk_istd = P.mark("act")
        P.wait("pe", "act", mk_istd)
        MM(ps[0][:, 0:512], ONES1[:], MRW[0:1, 0:512], True, True)
        MM(ps[1][:, 0:512], ONES1[:], MRW[0:1, 512:TB], True, True)
        MM(ps[2][:, 0:512], ONES1[:], VRW[0:1, 0:512], True, True)
        MM(ps[3][:, 0:512], ONES1[:], VRW[0:1, 512:TB], True, True)
        mk_bc = P.mark("pe")
        P.wait("act", "pe", mk_bc)
        ACT(MBF[:, 0:512], ps[0][:, 0:512], AF.Copy)
        ACT(MBF[:, 512:TB], ps[1][:, 0:512], AF.Copy)
        ACT(SBF[:, 0:512], ps[2][:, 0:512], AF.Copy)
        ACT(SBF[:, 512:TB], ps[3][:, 0:512], AF.Copy)
        mk_bce = P.mark("act")
        P.wait("dve", "act", mk_bce)
        for k in range(DT_K):
            TT(sq[:, 0:TB], XAB[:, k * TB:(k + 1) * TB], MBF[:, 0:TB],
               ALU.subtract)
            TT(XHB[:, k * TB:(k + 1) * TB], sq[:, 0:TB], SBF[:, 0:TB],
               ALU.mult)
        mk_xh = P.mark("dve")
        P.wait("pe", "dve", mk_xh)
        P.wait("act", "dve", mk_xh)

        def wslice_dma(wsrc, m, buf, kt, after_pe=None, sem="dwA"):
            P.wait("sp", "pe", after_pe)
            if buf is WSL2:
                bap = buf
            else:
                bap = buf[:]
            return P.dma("sp", lambda e, s=wsrc, mm=m, b=bap, K=kt:
                         e.dma_start(
                             out=b.rearrange("p (k n) -> p k n", n=128)
                             [:, 0:K, :],
                             in_=s.rearrange("(k p) n -> p k n", p=128)
                             [:, 0:K, mm * 128:(mm + 1) * 128]), sem)

        P.wait("sp", "dve", mk_xh)
        mk_gq = {}

        def w12_ld(m):
            q = (m % 2) * 2
            P.wait("sp", "pe", mk_gq.get(m - 2))
            l1 = P.dma("sp", lambda e, mm=m, qq=q: e.dma_start(
                out=WSL[:, qq * 1024:(qq + 1) * 1024].rearrange(
                    "p (k n) -> p k n", n=128),
                in_=w1_bf.rearrange("(k p) n -> p k n", p=128)
                [:, 0:DT_K, mm * 128:(mm + 1) * 128]), "dwA")
            l2 = P.dma("sp", lambda e, mm=m, qq=q: e.dma_start(
                out=WSL[:, (qq + 1) * 1024:(qq + 2) * 1024].rearrange(
                    "p (k n) -> p k n", n=128),
                in_=w2_bf.rearrange("(k p) n -> p k n", p=128)
                [:, 0:DT_K, mm * 128:(mm + 1) * 128]), "dwB")
            return (l1, l2)

        lw = {0: w12_ld(0)}
        for m in range(HF_K):
            if m + 1 < HF_K:
                lw[m + 1] = w12_ld(m + 1)
            q = (m % 2) * 2
            P.wait("pe", "dwA", lw[m][0])
            P.wait("pe", "dwB", lw[m][1])
            for k in range(DT_K):
                for nh in range(2):
                    MM(ps[0 + nh][:, 0:512],
                       WSL[:, q * 1024 + k * 128:q * 1024 + (k + 1) * 128],
                       XHB[:, k * TB + nh * 512:k * TB + (nh + 1) * 512],
                       k == 0, k == DT_K - 1)
            for k in range(DT_K):
                for nh in range(2):
                    MM(ps[2 + nh][:, 0:512],
                       WSL[:, (q + 1) * 1024 + k * 128:
                           (q + 1) * 1024 + (k + 1) * 128],
                       XHB[:, k * TB + nh * 512:k * TB + (nh + 1) * 512],
                       k == 0, k == DT_K - 1)
            mk_g = P.mark("pe")
            mk_gq[m] = mk_g
            if False:
                pass
            P.wait("act", "pe", mk_g)
            ACT(sq[:, 0:512], ps[0][:, 0:512], AF.Silu, scale=FS1[:, m:m + 1])
            ACT(sq[:, 512:TB], ps[1][:, 0:512], AF.Silu, scale=FS1[:, m:m + 1])
            ACT(sq[:, TB:TB + 512], ps[2][:, 0:512], AF.Identity,
                scale=FS2[:, m:m + 1])
            ACT(sq[:, TB + 512:2 * TB], ps[3][:, 0:512], AF.Identity,
                scale=FS2[:, m:m + 1])
            mk_t = P.mark("act")
            P.wait("dve", "act", mk_t)
            TT(GBUF[:, m * TB:(m + 1) * TB], sq[:, 0:TB], sq[:, TB:2 * TB],
               ALU.mult)
            mk_gd = P.mark("dve")
            P.wait("pe", "dve", mk_gd)
            P.wait("act", "dve", mk_gd)
        mk_h = P.mark("pe") if False else P.cnt["pe"]
        P.wait("sp", "dve", mk_gd)

        WSLB = XHB[:, 0:HF_K * 128]
        mk_evac = {}

        def stage(wsrc, out_fn, n_m, rd_buf, scale_tab, last_stage=False):
            nonlocal mk_h
            lds = {0: wslice_dma(wsrc, 0, WSL, HF_K, after_pe=mk_h,
                                 sem="dwA")}
            lds[1] = wslice_dma(wsrc, 1, WSLB, HF_K, after_pe=mk_h,
                                sem="dwB") if n_m > 1 else None
            for m in range(n_m):
                wbuf = WSL if m % 2 == 0 else WSLB
                wsem = "dwA" if m % 2 == 0 else "dwB"
                pban = (m % 2) * 2
                P.wait("pe", wsem, lds[m])
                P.wait("pe", "act" if not last_stage else "dve",
                       mk_evac.get(m - 2))
                for nh in range(2):
                    for k in range(HF_K):
                        MM(ps[pban + nh][:, 0:512],
                           wbuf[:, k * 128:(k + 1) * 128],
                           rd_buf[:, k * TB + nh * 512:k * TB + (nh + 1) * 512],
                           k == 0, k == HF_K - 1)
                mk_h = P.mark("pe")
                if m + 2 < n_m:
                    lds[m + 2] = wslice_dma(
                        wsrc, m + 2, WSL if m % 2 == 0 else WSLB, HF_K,
                        after_pe=mk_h, sem=wsem)
                out_fn(m, pban)
                mk_evac[m] = P.mark("act" if not last_stage else "dve")

        def h1_out(m, pban):
            P.wait("act", "pe", mk_h)
            ACT(HOUT[:, m * TB:m * TB + 512], ps[pban][:, 0:512], AF.Gelu,
                scale=FH1[:, m:m + 1])
            ACT(HOUT[:, m * TB + 512:(m + 1) * TB], ps[pban + 1][:, 0:512],
                AF.Gelu, scale=FH1[:, m:m + 1])

        stage(h1_bf, h1_out, HF_K, GBUF, FH1)

        def h2_out(m, pban):
            P.wait("act", "pe", mk_h)
            ACT(GBUF[:, m * TB:m * TB + 512], ps[pban][:, 0:512], AF.Gelu,
                scale=FH2[:, m:m + 1])
            ACT(GBUF[:, m * TB + 512:(m + 1) * TB], ps[pban + 1][:, 0:512],
                AF.Gelu, scale=FH2[:, m:m + 1])

        mk_evac = {}
        stage(h2_bf, h2_out, HF_K, HOUT, FH2)
        mk_yo = None

        def w3_out(m, pban):
            P.wait("dve", "pe", mk_h)
            STT(XAB[:, m * TB:m * TB + 512], ps[pban][:, 0:512],
                FS3[:, m:m + 1], XAB[:, m * TB:m * TB + 512],
                ALU.mult, ALU.add)
            STT(XAB[:, m * TB + 512:(m + 1) * TB], ps[pban + 1][:, 0:512],
                FS3[:, m:m + 1], XAB[:, m * TB + 512:(m + 1) * TB],
                ALU.mult, ALU.add)

        mk_evac = {}
        stage(w3_bf, w3_out, DT_K, GBUF, FS3, last_stage=True)
        mk_yo = P.cnt["dve"]
        mk_p3_dve = mk_yo
        P.wait("sp", "dve", mk_yo)
        out_done = P.dma("sp", lambda e, tt=t0: e.dma_start(
            out=yT.rearrange("(k p) n -> p k n", p=128)[:, :, tt:tt + TB],
            in_=XAB[:].rearrange("p (k n) -> p k n", n=TB)), "dy")

    for name, dout in dbg_outs.items():
        srct = dbg_srcs[name]
        out_done = P.dma("sp", lambda e, d=dout, s=srct: e.dma_start(
            out=d, in_=s), "dy")

    if out_done is not None:
        P.wait("sp", "dy", out_done)
    for _sem, _cnt in sorted(P.dma_cnt.items()):
        P.wait("sp", _sem, _cnt)
    st_3.close()

    with stack:
        semobjs = {}
        for s_ in sorted(P.sem_names):
            semobjs[s_] = stack.enter_context(nc.semaphore(s_))
        block = stack.enter_context(nc.Block())
        P.emit(nc, block, semobjs)
    return nc


def _pack_feat(v, ntiles):
    return np.ascontiguousarray(np.asarray(v, dtype=np.float32)
                                .reshape(ntiles, 128).T)


def make_inputs(core, I):
    f32 = np.float32
    xb = np.ascontiguousarray(np.asarray(I["L0_attn_x"][core], dtype=f32).T)
    selB = np.zeros((32, 16 * 128), dtype=ml_dtypes.bfloat16)
    selC = np.zeros((32, 16 * 128), dtype=ml_dtypes.bfloat16)
    for s in range(16):
        selB[s, s * 128:(s + 1) * 128] = 1
        selC[16 + s, s * 128:(s + 1) * 128] = 1
    d = {
        "xT": xb,
        "ipw": np.ascontiguousarray(np.asarray(I["L0_attn_in_proj_w"], f32).T),
        "xpw": np.ascontiguousarray(np.asarray(I["L0_attn_x_proj_w"], f32).T),
        "dtw": np.ascontiguousarray(np.asarray(I["L0_attn_dt_proj_w"], f32).T),
        "opw": np.ascontiguousarray(np.asarray(I["L0_attn_out_proj_w"], f32).T),
        "wwt": np.ascontiguousarray(np.asarray(I["L0_attn_proj_write_w"], f32).T),
        "wrt": np.ascontiguousarray(np.asarray(I["L0_attn_proj_read_w"], f32).T),
        "w1t": np.ascontiguousarray(np.asarray(I["L0_ffn_w1_w"], f32).T),
        "w2t": np.ascontiguousarray(np.asarray(I["L0_ffn_w2_w"], f32).T),
        "h1t": np.ascontiguousarray(np.asarray(I["L0_ffn_h1_w"], f32).T),
        "h2t": np.ascontiguousarray(np.asarray(I["L0_ffn_h2_w"], f32).T),
        "w3t": np.ascontiguousarray(np.asarray(I["L0_ffn_w3_w"], f32).T),
        "ipb_h": _pack_feat(I["L0_attn_in_proj_b"], 32),
        "convw_h": np.ascontiguousarray(
            np.asarray(I["L0_attn_conv_w"], f32).reshape(16, 128, 4)
            .transpose(1, 0, 2).reshape(128, 64)),
        "convb_h": _pack_feat(I["L0_attn_conv_b"], 16),
        "dtb_h": _pack_feat(I["L0_attn_dt_proj_b"], 16),
        "Alog_h": np.ascontiguousarray(
            np.asarray(I["L0_attn_A_log"], f32).reshape(16, 128, 16)
            .transpose(1, 0, 2).reshape(128, 256)),
        "Dssm_h": _pack_feat(I["L0_attn_D_ssm"], 16),
        "opb_h": _pack_feat(I["L0_attn_out_proj_b"], 8),
        "n1_h": _pack_feat(I["L0_attn_norm1_w"], 8),
        "n2_h": _pack_feat(I["L0_attn_norm2_w"], 8),
        "w1s_h": _pack_feat(I["L0_ffn_w1_s"], 32),
        "w2s_h": _pack_feat(I["L0_ffn_w2_s"], 32),
        "h1s_h": _pack_feat(I["L0_ffn_h1_s"], 32),
        "h2s_h": _pack_feat(I["L0_ffn_h2_s"], 32),
        "w3s_h": _pack_feat(I["L0_ffn_w3_s"], 8),
        "decay_h": np.asarray(I["L0_attn_decay"], dtype=f32).reshape(1, 1),
        "ones1_h": np.ones((1, 128), f32),
        "onesk_h": np.ones((128, 1), f32),
        "iota_h": np.arange(128, dtype=f32).reshape(128, 1),
        "iotar_h": (127 - np.arange(128, dtype=f32)).reshape(128, 1),
        "midx_h": (np.arange(128)[None, :] - 1 -
                   np.arange(128)[:, None]).astype(f32),
        "caus_h": (np.arange(128)[None, :] >
                   np.arange(128)[:, None]).astype(f32),
        "eye_h": np.eye(128, dtype=f32),
        "eyebf_h": np.eye(128, dtype=ml_dtypes.bfloat16),
        "selB_h": selB,
        "selC_h": selC,
        "cvals_h": np.broadcast_to(
            np.array([1.0, -1.0, -0.5, 128.0, 1.0 / 1024, 1e-5, 0.1, 0.0],
                     dtype=f32), (128, 8)).copy(),
    }
    return d


def kernel(**inputs):
    nc = build_nc()
    in_maps = [make_inputs(c, inputs) for c in range(8)]
    res = run_bass_kernel_spmd(nc, in_maps, core_ids=list(range(8)))
    out = np.stack([np.asarray(r["yT"]).T for r in res.results], axis=0)
    return np.ascontiguousarray(out.astype(np.float32))


# version 09 (final)
# speedup vs baseline: 1.1684x; 1.1684x over previous
import numpy as np
import ml_dtypes
import concourse.bass as bass
import concourse.mybir as mybir
import concourse.bass_utils as _bass_utils
from concourse.bass_utils import run_bass_kernel_spmd

_orig_run_command = _bass_utils.run_command

def _run_command_ldwopt(cmd, *a, **kw):
    if isinstance(cmd, list):
        cmd = ["--enable-ldw-opt=true" if c == "--enable-ldw-opt=false" else c
               for c in cmd]
    return _orig_run_command(cmd, *a, **kw)

_bass_utils.run_command = _run_command_ldwopt

F32 = mybir.dt.float32
BF16 = mybir.dt.bfloat16
AF = mybir.ActivationFunctionType
ALU = mybir.AluOpType
AX = mybir.AxisListType

B, T, D, DI, DS, DTR, HF = 8, 4096, 1024, 2048, 16, 64, 4096
TC = 512
NSEG = T // TC
CH = 128
NCH = TC // CH
TB = 1024
NTB = T // TB
ALPHA = 0.1
EPS = 1e-5

DT_K = D // 128
DI_K = DI // 128
HF_K = HF // 128


class Prog:
    ENGS = ("pe", "act", "dve", "pool", "sp")

    def __init__(self):
        self.phase_marks = []
        self.ops = {e: [] for e in self.ENGS}
        self.cnt = {e: 0 for e in self.ENGS}
        self.dma_cnt = {}
        self.last_wait = {}
        self.sem_names = set(self.ENGS)

    def op(self, e, fn):
        self.ops[e].append([fn, []])

    def mark(self, e):
        assert self.ops[e], f"mark on empty stream {e}"
        self.cnt[e] += 1
        self.ops[e][-1][1].append((e, 1))
        return self.cnt[e]

    def dma(self, queue, fn, sem_name):
        self.sem_names.add(sem_name)
        self.dma_cnt[sem_name] = self.dma_cnt.get(sem_name, 0) + 16
        self.ops[queue].append([fn, [(sem_name, 16)]])
        return self.dma_cnt[sem_name]

    def wait(self, consumer, sem_name, count):
        if count is None:
            return
        k = (consumer, sem_name)
        if self.last_wait.get(k, -1) >= count:
            return
        self.last_wait[k] = count
        self.ops[consumer].append([("WAIT", sem_name, count), []])

    def emit(self, nc, block, sems):
        handles = {"pe": nc.tensor, "act": nc.scalar, "dve": nc.vector,
                   "pool": nc.gpsimd, "sp": nc.sync}
        deco = {"pe": block.tensor, "act": block.scalar, "dve": block.vector,
                "pool": block.gpsimd, "sp": block.sync}
        for e in self.ENGS:
            ops = self.ops[e]
            eng = handles[e]

            def section(_eng=eng, _ops=ops):
                for fn, incs in _ops:
                    if isinstance(fn, tuple) and fn[0] == "WAIT":
                        _eng.wait_ge(sems[fn[1]], fn[2])
                        continue
                    inst = fn(_eng)
                    for (sem_name, step) in incs:
                        inst.then_inc(sems[sem_name], step)

            deco[e](section)


def build_nc(dbg=(), stop_after=None):
    import contextlib
    nc = bass.Bass("TRN2", target_bir_lowering=False)
    nc.detect_race_conditions = False
    P = Prog()
    stack = contextlib.ExitStack()

    def din(name, shape, dt=F32):
        return nc.dram_tensor(name, shape, dt, kind="ExternalInput").ap()

    xT = din("xT", [D, T])
    ipw = din("ipw", [D, 2 * DI])
    xpw = din("xpw", [DI, 96])
    dtw = din("dtw", [DTR, DI])
    opw = din("opw", [DI, D])
    wwt = din("wwt", [D, D])
    wrt = din("wrt", [D, D])
    w1t = din("w1t", [D, HF])
    w2t = din("w2t", [D, HF])
    h1t = din("h1t", [HF, HF])
    h2t = din("h2t", [HF, HF])
    w3t = din("w3t", [HF, D])
    ipb_h = din("ipb_h", [128, 32])
    convw_h = din("convw_h", [128, 64])
    convb_h = din("convb_h", [128, 16])
    dtb_h = din("dtb_h", [128, 16])
    Alog_h = din("Alog_h", [128, 256])
    Dssm_h = din("Dssm_h", [128, 16])
    opb_h = din("opb_h", [128, 8])
    n1_h = din("n1_h", [128, 8])
    n2_h = din("n2_h", [128, 8])
    w1s_h = din("w1s_h", [128, 32])
    w2s_h = din("w2s_h", [128, 32])
    h1s_h = din("h1s_h", [128, 32])
    h2s_h = din("h2s_h", [128, 32])
    w3s_h = din("w3s_h", [128, 8])
    decay_h = din("decay_h", [1, 1])
    ones1_h = din("ones1_h", [1, 128])
    onesk_h = din("onesk_h", [128, 1])
    iota_h = din("iota_h", [128, 1])
    iotar_h = din("iotar_h", [128, 1])
    midx_h = din("midx_h", [128, 128])
    caus_h = din("caus_h", [128, 128])
    eye_h = din("eye_h", [128, 128])
    eyebf_h = din("eyebf_h", [128, 128], BF16)
    selB_h = din("selB_h", [32, 16 * 128], BF16)
    selC_h = din("selC_h", [32, 16 * 128], BF16)
    cvals_h = din("cvals_h", [128, 8])

    yT = nc.dram_tensor("yT", [D, T], F32, kind="ExternalOutput").ap()

    def scratch(name, shape, dt):
        return nc.dram_tensor(name, shape, dt, kind="Internal").ap()

    ipw_bf = scratch("ipw_bf", [D, 2 * DI], BF16)
    xpw_bf_d = scratch("xpw_bf_d", [DI, 96], BF16)
    dtw_bf_d = scratch("dtw_bf_d", [DTR, DI], BF16)
    opw_bf = scratch("opw_bf", [DI, D], BF16)
    ww_bf = scratch("ww_bf", [D, D], BF16)
    wr_bf = scratch("wr_bf", [D, D], BF16)
    w1_bf = scratch("w1_bf", [D, HF], BF16)
    w2_bf = scratch("w2_bf", [D, HF], BF16)
    h1_bf = scratch("h1_bf", [HF, HF], BF16)
    h2_bf = scratch("h2_bf", [HF, HF], BF16)
    w3_bf = scratch("w3_bf", [HF, D], BF16)
    x1_sp = scratch("x1_sp", [D, T], F32)
    out2_sp = scratch("out2_sp", [D, T], F32)
    xA_sp = scratch("xA_sp", [D, T], F32)
    xm_sp = scratch("xm_sp", [DI, T], BF16)
    sil_sp = scratch("sil_sp", [DI, T], BF16)

    dbg_srcs = {"x1": x1_sp, "out2": out2_sp, "xA": xA_sp,
                "xm": xm_sp, "sil": sil_sp}
    dbg_outs = {}
    for name in dbg:
        src = dbg_srcs[name]
        dbg_outs[name] = nc.dram_tensor(
            "dbg_" + name, list(src.shape), src.dtype,
            kind="ExternalOutput").ap()

    def mksb(st, name, shape, dt):
        return st.enter_context(nc.sbuf_tensor(name, shape, dt))

    def sbuf(name, shape, dt):
        return mksb(stack, name, shape, dt)

    IPB = sbuf("IPB", [128, 32], F32)
    CONVW = sbuf("CONVW", [128, 64], F32)
    CONVB = sbuf("CONVB", [128, 16], F32)
    DTB = sbuf("DTB", [128, 16], F32)
    ASB = sbuf("ASB", [128, 256], F32)
    DSSM = sbuf("DSSM", [128, 16], F32)
    OPB = sbuf("OPB", [128, 8], F32)
    N1W = sbuf("N1W", [128, 8], F32)
    N2W = sbuf("N2W", [128, 8], F32)
    FS1 = sbuf("FS1", [128, 32], F32)
    FS2 = sbuf("FS2", [128, 32], F32)
    FH1 = sbuf("FH1", [128, 32], F32)
    FH2 = sbuf("FH2", [128, 32], F32)
    FS3 = sbuf("FS3", [128, 8], F32)
    ONES1 = sbuf("ONES1", [1, 128], F32)
    ONESK = sbuf("ONESK", [128, 1], F32)
    IOTA = sbuf("IOTA", [128, 1], F32)
    IOTAR = sbuf("IOTAR", [128, 1], F32)
    MIDX = sbuf("MIDX", [128, 128], F32)
    CAUS = sbuf("CAUS", [128, 128], F32)
    EYE = sbuf("EYE", [128, 128], F32)
    EYEBF = sbuf("EYEBF", [128, 128], BF16)
    SELB = sbuf("SELB", [32, 16 * 128], BF16)
    SELC = sbuf("SELC", [32, 16 * 128], BF16)
    DEC = sbuf("DEC", [1, 4], F32)
    LNG = sbuf("LNG", [128, 1], F32)
    GIN = sbuf("GIN", [128, 1], F32)
    GW = sbuf("GW", [128, 1], F32)
    GC = sbuf("GC", [128, 1], F32)
    MT = sbuf("MT", [128, 128], F32)
    GINR = sbuf("GINR", [128, 128], F32)
    CVAL = sbuf("CVAL", [128, 8], F32)
    _CONST_VALS = [1.0, -1.0, -0.5, 128.0, 1.0 / D, EPS, ALPHA, 0.0]
    for _j, _v in enumerate(_CONST_VALS):
        nc.const_aps.register(F32, _v, CVAL[:, _j:_j + 1])
    SCAR = sbuf("SCAR", [128, 16 * 16], F32)
    CCAR = sbuf("CCAR", [128, 16 * 3], BF16)
    GROW = sbuf("GROW", [1, 128], F32)
    import contextlib as _ctx
    st_pro = _ctx.ExitStack()
    STG0 = mksb(st_pro, "STG0", [128, 2048], BF16)
    st_m = _ctx.ExitStack()

    def alloc_mamba():
        g = {}
        g["WS"] = mksb(st_m, "WS", [128, DT_K * 1024], BF16)
        g["XPW"] = mksb(st_m, "XPW", [128, DI_K * 96], BF16)
        g["DTW"] = mksb(st_m, "DTW", [64, DI], BF16)
        g["XSEG"] = mksb(st_m, "XSEG", [128, DT_K * TC], F32)
        g["H0B"] = mksb(st_m, "H0B", [128, DT_K * TC], BF16)
        g["XIN"] = mksb(st_m, "XIN", [128, DT_K * TC], BF16)
        g["SCR"] = mksb(st_m, "SCR", [128, 7 * 512], F32)
        g["BIL"] = mksb(st_m, "BIL", [128, TC * 16], BF16)
        g["CIL"] = mksb(st_m, "CIL", [128, TC * 16], BF16)
        g["DBUIL"] = mksb(st_m, "DBUIL", [128, TC * 16], BF16)
        g["HIL"] = mksb(st_m, "HIL", [128, TC * 16], F32)
        g["YBF"] = mksb(st_m, "YBF", [128, DI_K * TC], BF16)
        g["XMD"] = mksb(st_m, "XMD", [128, 2 * TC], BF16)
        g["SLD"] = mksb(st_m, "SLD", [128, 2 * TC], BF16)
        g["XDBL0"] = mksb(st_m, "XDBL0", [64, TC], BF16)
        g["XDBL1"] = mksb(st_m, "XDBL1", [64, TC], BF16)
        g["BCR"] = mksb(st_m, "BCR", [32, TC], BF16)
        g["MBX"] = mksb(st_m, "MBX", [128, TC], F32)
        g["SBX"] = mksb(st_m, "SBX", [128, TC], F32)
        g["CBX"] = mksb(st_m, "CBX", [128, TC], F32)
        g["XZM"] = mksb(st_m, "XZM", [128, TC + 3], BF16)
        g["CONVACC"] = mksb(st_m, "CONVACC", [128, TC], F32)
        g["FSCR"] = mksb(st_m, "FSCR", [128, TC], F32)
        g["OPSL"] = mksb(st_m, "OPSL", [128, 2 * TC], F32)
        g["XMST"] = g["FSCR"][:].bitcast(BF16)
        g["YPB"] = g["HIL"][:, 0:TC * 8].bitcast(BF16)
        S = g["SCR"]
        g["DELTA0"] = S[:, 0 * 512:1 * 512]
        g["DELTA1"] = S[:, 1 * 512:2 * 512]
        g["SPT"] = S[:, 2 * 512:3 * 512]
        g["DA0"] = S[:, 3 * 512:4 * 512]
        g["DA1"] = S[:, 4 * 512:5 * 512]
        g["YSSM"] = S[:, 5 * 512:6 * 512]
        g["DU0"] = S[:, 6 * 512:6 * 512 + 256].bitcast(BF16)
        g["DU1"] = S[:, 6 * 512 + 256:7 * 512].bitcast(BF16)
        return g

    ps = [stack.enter_context(nc.psum_tensor(f"ps{i}", [128, 512], F32))
          for i in range(7)]
    psbf = stack.enter_context(nc.psum_tensor("psbf", [128, 512], BF16))

    def MM(out_ap, lhsT_ap, rhs_ap, start, stop):
        P.op("pe", lambda t, o=out_ap, a=lhsT_ap, b=rhs_ap, s1=start, s2=stop:
             t.matmul(o, a, b, start=s1, stop=s2))

    def ACT(out_ap, in_ap, func, bias=None, scale=None):
        kw = {}
        if bias is not None:
            kw["bias"] = bias
        if scale is not None:
            kw["scale"] = scale
        P.op("act", lambda a, o=out_ap, i=in_ap, f=func, k=kw:
             a.activation(o, i, f, **k))

    def TT(out_ap, a_ap, b_ap, op):
        P.op("dve", lambda v, o=out_ap, x=a_ap, y=b_ap, p_=op:
             v.tensor_tensor(o, x, y, p_))

    def STT(out_ap, in0_ap, scal, in1_ap, op0, op1):
        P.op("dve", lambda v, o=out_ap, i0=in0_ap, s=scal, i1=in1_ap,
             p0=op0, p1=op1: v.scalar_tensor_tensor(o, i0, s, i1, p0, p1))

    cl = None
    for dst, src in [(IPB, ipb_h), (CONVW, convw_h), (CONVB, convb_h),
                     (DTB, dtb_h), (DSSM, Dssm_h), (OPB, opb_h),
                     (N1W, n1_h), (N2W, n2_h), (FS1, w1s_h), (FS2, w2s_h),
                     (FH1, h1s_h), (FH2, h2s_h), (FS3, w3s_h),
                     (ONES1, ones1_h), (ONESK, onesk_h), (IOTA, iota_h),
                     (IOTAR, iotar_h), (MIDX, midx_h), (CAUS, caus_h),
                     (EYE, eye_h), (EYEBF, eyebf_h), (SELB, selB_h),
                     (SELC, selC_h)]:
        cl = P.dma("sp", lambda e, d=dst, s=src: e.dma_start(out=d[:], in_=s),
                   "dma_a")
    cl = P.dma("sp", lambda e: e.dma_start(out=CVAL[:], in_=cvals_h), "dma_a")
    cl = P.dma("sp", lambda e: e.dma_start(out=DEC[0:1, 0:1], in_=decay_h),
               "dma_a")
    cl = P.dma("sp", lambda e: e.dma_start(out=ASB[:], in_=Alog_h), "dma_a")

    P.wait("act", "dma_a", cl)
    ACT(ASB[:], ASB[:], AF.Exp)
    ACT(ASB[:], ASB[:], AF.Copy, scale=-1.0)
    ACT(DEC[0:1, 1:2], DEC[0:1, 0:1], AF.Sigmoid)
    ACT(DEC[0:1, 2:3], DEC[0:1, 1:2], AF.Ln)
    mk_act = P.mark("act")
    P.wait("pe", "act", mk_act)
    MM(ps[0][:, 0:1], ONES1[:], DEC[0:1, 2:3], True, True)
    mk_pe = P.mark("pe")
    P.wait("act", "pe", mk_pe)
    ACT(LNG[:], ps[0][:, 0:1], AF.Copy)
    ACT(GIN[:], LNG[:], AF.Exp, scale=IOTA[:, 0:1])
    ACT(GW[:], LNG[:], AF.Exp, scale=IOTAR[:, 0:1])
    ACT(GC[:], LNG[:], AF.Exp, scale=128.0)
    ACT(MT[:], MIDX[:], AF.Exp, scale=LNG[:, 0:1])
    mk_act = P.mark("act")
    P.wait("dve", "act", mk_act)
    TT(MT[:], MT[:], CAUS[:], ALU.mult)
    P.wait("pe", "act", mk_act)
    MM(ps[1][0:1, 0:128], GIN[:], EYE[:], True, True)
    mk_pe = P.mark("pe")
    P.wait("act", "pe", mk_pe)
    ACT(GROW[0:1, 0:128], ps[1][0:1, 0:128], AF.Copy)
    mk_act = P.mark("act")
    P.wait("pe", "act", mk_act)
    MM(ps[1][:, 0:128], ONES1[:], GROW[0:1, 0:128], True, True)
    mk_pe = P.mark("pe")
    P.wait("act", "pe", mk_pe)
    ACT(GINR[:], ps[1][:, 0:128], AF.Copy)
    mk_ginr = P.mark("act")

    P.op("dve", lambda v: v.memset(SCAR[:], 0.0))
    P.op("dve", lambda v: v.memset(CCAR[:], 0.0))
    mk_zero = P.mark("dve")

    stage_last_store = {0: None, 1: None}
    cast_idx = [0]

    def cast_weight(src, dst, rows, cols):
        for r in range(0, rows, 128):
            rr = min(128, rows - r)
            for c0 in range(0, cols, 2048):
                cc = min(2048, cols - c0)
                stage = STG0
                P.wait("pool", "ds0", stage_last_store[0])
                ld = P.dma("pool",
                           lambda e, s=src, st=stage, r0=r, rr_=rr, cb=c0,
                           c=cc: e.dma_start(
                               out=st[0:rr_, 0:c],
                               in_=s[r0:r0 + rr_, cb:cb + c]), "dp0")
                P.wait("sp", "dp0", ld)
                stc = P.dma("sp",
                            lambda e, d=dst, st=stage, r0=r, rr_=rr, cb=c0,
                            c=cc: e.dma_start(
                                out=d[r0:r0 + rr_, cb:cb + c],
                                in_=st[0:rr_, 0:c]), "ds0")
                stage_last_store[0] = stc
                cast_idx[0] += 1
        return None

    for (s_, d_, r_, c_) in [
        (ipw, ipw_bf, D, 2 * DI), (xpw, xpw_bf_d, DI, 96),
        (dtw, dtw_bf_d, DTR, DI), (opw, opw_bf, DI, D),
    ]:
        cast_done = cast_weight(s_, d_, r_, c_)
    ffn_cast_chunks = []
    for (s_, d_, r_, c_) in [
        (w1t, w1_bf, D, HF), (w2t, w2_bf, D, HF),
        (h1t, h1_bf, HF, HF), (h2t, h2_bf, HF, HF), (w3t, w3_bf, HF, D),
    ]:
        for r in range(0, r_, 128):
            rr = min(128, r_ - r)
            for c0 in range(0, c_, 2048):
                cc = min(2048, c_ - c0)
                ffn_cast_chunks.append((s_, d_, r, rr, c0, cc))

    def emit_ffn_casts(n):
        for _ in range(n):
            if not ffn_cast_chunks:
                return
            s_, d_, r0, rr, c0, cc = ffn_cast_chunks.pop(0)
            stage = STG0
            P.wait("pool", "ds0", stage_last_store[0])
            ld = P.dma("pool",
                       lambda e, s=s_, st=stage, rb=r0, rr_=rr, cb=c0, c=cc:
                       e.dma_start(out=st[0:rr_, 0:c],
                                   in_=s[rb:rb + rr_, cb:cb + c]), "dp0")
            P.wait("act", "dp0", ld)
            stc = P.dma("act",
                        lambda e, d=d_, st=stage, rb=r0, rr_=rr, cb=c0, c=cc:
                        e.dma_start(out=d[rb:rb + rr_, cb:cb + c],
                                    in_=st[0:rr_, 0:c]), "ds0")
            stage_last_store[0] = stc
            cast_idx[0] += 1

    P.wait("sp", "ds0", P.dma_cnt.get("ds0"))
    P.wait("sp", "ds1", P.dma_cnt.get("ds1"))
    g = alloc_mamba()
    WS, XPW, DTW = g["WS"], g["XPW"], g["DTW"]
    XSEG, H0B, XIN, SCR = g["XSEG"], g["H0B"], g["XIN"], g["SCR"]
    XZM, CONVACC, BCR = g["XZM"], g["CONVACC"], g["BCR"]
    XDBLP = (g["XDBL0"], g["XDBL1"])
    BIL, CIL = g["BIL"], g["CIL"]
    MBX, SBX, CBX = g["MBX"], g["SBX"], g["CBX"]
    FSCR, XMST, OPSL = g["FSCR"], g["XMST"], g["OPSL"]
    XMD, SLD = g["XMD"], g["SLD"]
    DBUIL, YBF = g["DBUIL"], g["YBF"]
    HIL, YPB = g["HIL"], g["YPB"]
    DELTA0, DELTA1, SPT = g["DELTA0"], g["DELTA1"], g["SPT"]
    DA0, DA1, YSSM = g["DA0"], g["DA1"], g["YSSM"]
    DU0, DU1 = g["DU0"], g["DU1"]
    R0 = OPSL[0:1, 0:TC]
    R1 = OPSL[0:1, TC:2 * TC]
    C0R = CONVACC[0:1, 0:TC]
    F0R = FSCR[0:1, 0:TC]
    P.dma("sp", lambda e: e.dma_start(
        out=XPW[:].rearrange("p (k n) -> p k n", n=96),
        in_=xpw_bf_d.rearrange("(k p) n -> p k n", p=128)), "dres")
    ld_res = P.dma("sp", lambda e: e.dma_start(out=DTW[:], in_=dtw_bf_d),
                   "dres")

    marks = {
        "bil_consumed_pool": None, "il_done": None, "selmm_done": None,
        "last_sp_act": None, "ws_pe_done": None,
        "oproj_evac_act": None, "oproj_evac_dve": None,
        "oproj_spill": {"dsp1": None, "dsp3": None},
        "h0b_done_pool": None, "stats_pe_done": None, "xin_done": None,
        "stats_evac_act": None, "inproj_evac_act": None,
        "x1_spill_total": None, "out2_spill_total": None,
        "scan_gate_done": None, "init_zero_dve": mk_zero,
        "dres_ld": ld_res,
    }

    def ckpt(tag):
        P.phase_marks.append((tag, {e: len(P.ops[e]) for e in P.ENGS}))

    def front_steps(gseg, pid, first_pass_reset=False):
        nw = N1W if pid == 0 else N2W
        t0 = gseg * TC
        src = xT if pid == 0 else x1_sp
        if first_pass_reset:
            P.wait("dve", "act", P.cnt["act"])
            P.op("dve", lambda v: v.memset(CCAR[:], 0.0))
            marks["ccar_reset"] = P.mark("dve")
        P.wait("sp", "pool", marks["h0b_done_pool"])
        P.wait("sp", "dve", marks["xin_done"])
        P.wait("sp", "pe", marks["stats_pe_done"])
        if pid == 1:
            P.wait("sp", "dsp1", marks.get(("x1_seg", gseg)))
        ldx = P.dma("sp", lambda e, tt=t0, s=src: e.dma_start(
            out=XSEG[:].rearrange("p (k n) -> p k n", n=TC),
            in_=s.rearrange("(k p) n -> p k n", p=128)[:, :, tt:tt + TC]),
            "dX")
        yield
        P.wait("act", "dX", ldx)
        P.wait("act", "pool", marks["h0b_done_pool"])
        for _s, _c in marks["oproj_spill"].items():
            P.wait("act", _s, _c)
            P.wait("dve", _s, _c)
        P.wait("pe", "act", marks["oproj_evac_act"])
        P.wait("pe", "dve", marks["oproj_evac_dve"])
        mmh = {}
        for k in range(DT_K):
            sl = OPSL[:, (k % 2) * TC:(k % 2) * TC + TC]
            P.wait("act", "pe", mmh.get(k - 2))
            ACT(sl, XSEG[:, k * TC:(k + 1) * TC], AF.Square)
            mk_a = P.mark("act")
            P.wait("pe", "act", mk_a)
            if pid == 0:
                MM(ps[0][0:1, 0:TC], ONESK[:], XSEG[:, k * TC:(k + 1) * TC],
                   k == 0, k == DT_K - 1)
            MM(ps[1][0:1, 0:TC], ONESK[:], sl, k == 0, k == DT_K - 1)
            mmh[k] = P.mark("pe")
            if k == 3:
                yield
        marks["stats_pe_done"] = mmh[DT_K - 1]
        P.wait("act", "pe", mmh[DT_K - 1])
        if pid == 0:
            ACT(R0, ps[0][0:1, 0:TC], AF.Copy, scale=1.0 / D)
            ACT(R1, ps[1][0:1, 0:TC], AF.Copy, scale=1.0 / D)
            ACT(C0R, R0, AF.Square)
            mk_a = P.mark("act")
            P.wait("dve", "act", mk_a)
            P.wait("dve", "pool", marks["h0b_done_pool"])
            TT(R1, R1, C0R, ALU.subtract)
            mk_d = P.mark("dve")
            P.wait("act", "dve", mk_d)
            ACT(C0R, R1, AF.Ln, bias=EPS)
            ACT(F0R, C0R, AF.Exp, scale=-1.0)
            mk_a = P.mark("act")
            P.wait("dve", "act", mk_a)
            TT(F0R, R1, F0R, ALU.mult)
            mk_d = P.mark("dve")
            P.wait("act", "dve", mk_d)
            ACT(R1, C0R, AF.Exp, scale=-0.5)
            ACT(C0R, F0R, AF.Ln, bias=EPS)
            ACT(F0R, C0R, AF.Exp, scale=-0.5)
            mk_a = P.mark("act")
            P.wait("dve", "act", mk_a)
            TT(C0R, R1, F0R, ALU.mult)
            mk_d = P.mark("dve")
            P.wait("pe", "act", mk_a)
            P.wait("pe", "dve", mk_d)
            MM(ps[0][:, 0:TC], ONES1[:], R0, True, True)
            MM(ps[1][:, 0:TC], ONES1[:], R1, True, True)
            MM(ps[2][:, 0:TC], ONES1[:], C0R, True, True)
            mk_b = P.mark("pe")
            P.wait("act", "pe", mk_b)
            ACT(MBX[:, 0:TC], ps[0][:, 0:TC], AF.Copy)
            ACT(SBX[:, 0:TC], ps[1][:, 0:TC], AF.Copy)
            ACT(CBX[:, 0:TC], ps[2][:, 0:TC], AF.Copy)
            mk_bc = P.mark("act")
            marks["stats_evac_act"] = mk_bc
            yield
            P.wait("dve", "act", mk_bc)
            for k in range(DT_K):
                TT(FSCR[:, 0:TC], XSEG[:, k * TC:(k + 1) * TC], MBX[:, 0:TC],
                   ALU.subtract)
                STT(XIN[:, k * TC:(k + 1) * TC], FSCR[:, 0:TC], nw[:, k:k + 1],
                    CBX[:, 0:TC], ALU.mult, ALU.mult)
            mk_xin = P.mark("dve")
        else:
            ACT(R1, ps[1][0:1, 0:TC], AF.Copy, scale=1.0 / D)
            ACT(R1, R1, AF.Ln, bias=EPS)
            ACT(R1, R1, AF.Exp, scale=-0.5)
            mk_a = P.mark("act")
            P.wait("pe", "act", mk_a)
            MM(ps[1][:, 0:TC], ONES1[:], R1, True, True)
            mk_b = P.mark("pe")
            P.wait("act", "pe", mk_b)
            ACT(SBX[:, 0:TC], ps[1][:, 0:TC], AF.Copy)
            mk_bc = P.mark("act")
            marks["stats_evac_act"] = mk_bc
            yield
            P.wait("dve", "act", mk_bc)
            for k in range(DT_K):
                STT(XIN[:, k * TC:(k + 1) * TC], XSEG[:, k * TC:(k + 1) * TC],
                    nw[:, k:k + 1], SBX[:, 0:TC], ALU.mult, ALU.mult)
            mk_xin = P.mark("dve")
        marks["xin_done"] = mk_xin
        yield
        P.wait("pe", "dve", mk_xin)
        P.wait("pe", "dres", marks["dres_ld"])
        if pid == 0 and gseg == 0:
            P.wait("act", "dve", marks["init_zero_dve"])
        mk_e8 = {}
        slot_guard = {0: [], 1: []}
        mk_xdbl_mm = None

        def ld_e8(e8):
            half = e8 % 2
            P.wait("sp", "pe", mk_e8.get(e8 - 2, marks["ws_pe_done"]))
            return P.dma("sp", lambda e, ee=e8, hh=half: e.dma_start(
                out=WS[:, hh * 4096:(hh + 1) * 4096].rearrange(
                    "p (k n) -> p k n", n=512),
                in_=ipw_bf.rearrange("(k p) n -> p k n", p=128)
                [:, :, ee * 512:(ee + 1) * 512]), "dW")

        lde = {0: ld_e8(0)}
        for e8 in range(8):
            if e8 + 1 < 8:
                lde[e8 + 1] = ld_e8(e8 + 1)
            half = e8 % 2
            P.wait("pe", "dW", lde[e8])
            for mi in range(4):
                m = e8 * 4 + mi
                i = m if m < 16 else m - 16
                pb = ps[2 + (mi % 2)]
                for k in range(DT_K):
                    MM(pb[:, 0:TC],
                       WS[:, half * 4096 + k * 512 + mi * 128:
                          half * 4096 + k * 512 + mi * 128 + 128],
                       XIN[:].rearrange("p (k n) -> p k n", n=TC)[:, k, :],
                       k == 0, k == DT_K - 1)
                mk_mm = P.mark("pe")
                mk_e8[e8] = mk_mm
                P.wait("act", "pe", mk_mm)
                if m < 16:
                    if first_pass_reset and i == 0:
                        P.wait("act", "dve", marks["ccar_reset"])
                    ACT(XZM[:, 3:TC + 3], pb[:, 0:TC], AF.Identity,
                        bias=IPB[:, i:i + 1])
                    ACT(XZM[:, 0:3], CCAR[:, i * 3:i * 3 + 3], AF.Copy)
                    ACT(CONVACC[:, 0:TC], XZM[:, 0:TC], AF.Identity,
                        scale=CONVW[:, 4 * i:4 * i + 1])
                    mk_c = P.mark("act")
                    P.wait("dve", "act", mk_c)
                    for j in range(1, 4):
                        STT(CONVACC[:, 0:TC], XZM[:, j:TC + j],
                            CONVW[:, 4 * i + j:4 * i + j + 1],
                            CONVACC[:, 0:TC], ALU.mult, ALU.add)
                    mk_c2 = P.mark("dve")
                    P.wait("act", "dve", mk_c2)
                    for _s, _c in slot_guard[i % 2]:
                        P.wait("act", _s, _c)
                    slot = XMST[:, (i % 2) * TC:(i % 2) * TC + TC]
                    ACT(slot, CONVACC[:, 0:TC], AF.Silu,
                        bias=CONVB[:, i:i + 1])
                    ACT(CCAR[:, i * 3:i * 3 + 3], XZM[:, TC:TC + 3], AF.Copy)
                    mk_xm = P.mark("act")
                    P.wait("pe", "act", mk_xm)
                    MM(ps[4][0:96, 0:TC],
                       XPW[:].rearrange("p (k n) -> p k n", n=96)[:, i, :],
                       slot, i == 0, i == 15)
                    mk_xdbl_mm = P.mark("pe")
                    P.wait("sp", "act", mk_xm)
                    sp_mk = P.dma(
                        "sp", lambda e, ii=i, tt=t0, sl=slot: e.dma_start(
                            out=xm_sp[ii * 128:(ii + 1) * 128, tt:tt + TC],
                            in_=sl), "dxm")
                    slot_guard[i % 2] = [("dxm", sp_mk), ("pe", mk_xdbl_mm)]
                else:
                    for _s, _c in slot_guard[i % 2]:
                        P.wait("act", _s, _c)
                    slot = XMST[:, (i % 2) * TC:(i % 2) * TC + TC]
                    ACT(slot, pb[:, 0:TC], AF.Silu,
                        bias=IPB[:, 16 + i:16 + i + 1])
                    mk_s = P.mark("act")
                    P.wait("pe", "act", mk_s)
                    P.wait("sp", "act", mk_s)
                    sp_mk = P.dma(
                        "sp", lambda e, ii=i, tt=t0, sl=slot: e.dma_start(
                            out=sil_sp[ii * 128:(ii + 1) * 128, tt:tt + TC],
                            in_=sl), "dsl")
                    slot_guard[i % 2] = [("dsl", sp_mk)]
                yield
        marks["ws_pe_done"] = P.cnt["pe"]
        marks["inproj_evac_act"] = P.cnt["act"]
        marks[("xm_spilled", gseg)] = P.dma_cnt.get("dxm")
        marks[("sil_spilled", gseg)] = P.dma_cnt.get("dsl")
        marks[("xdbl_mm", gseg)] = mk_xdbl_mm

    def finish_front(gseg, pid):
        XDBL = XDBLP[gseg % 2]
        P.wait("act", "pe", marks[("xdbl_mm", gseg)])
        P.wait("act", "pe", marks["selmm_done"])
        P.wait("act", "pool", marks["bil_consumed_pool"])
        ACT(XDBL[:, 0:TC], ps[4][0:64, 0:TC], AF.Copy)
        ACT(BCR[:, 0:TC], ps[4][64:96, 0:TC], AF.Copy)
        mk_bcr = P.mark("act")
        marks[("xdbl_evac", gseg)] = mk_bcr
        P.wait("pe", "act", mk_bcr)
        P.wait("pe", "act", marks["last_sp_act"])
        ev_c = None
        for s in range(16):
            MM(ps[5][:, 0:TC], SELB[:, s * 128:(s + 1) * 128], BCR[:, 0:TC],
               True, True)
            mm_b = P.mark("pe")
            P.wait("act", "pe", mm_b)
            ACT(BIL[:].rearrange("p (t s) -> p t s", s=16)[:, :, s],
                ps[5][:, 0:TC], AF.Copy)
            ev_b = P.mark("act")
            MM(ps[6][:, 0:TC], SELC[:, s * 128:(s + 1) * 128], BCR[:, 0:TC],
               True, True)
            mm_c = P.mark("pe")
            P.wait("act", "pe", mm_c)
            ACT(CIL[:].rearrange("p (t s) -> p t s", s=16)[:, :, s],
                ps[6][:, 0:TC], AF.Copy)
            ev_c = P.mark("act")
            P.wait("pe", "act", ev_b)
            P.wait("pe", "act", ev_c)
        marks["selmm_done"] = P.cnt["pe"]
        marks["il_done"] = ev_c

    def emit_scan(gseg, pid, fit):
        t0 = gseg * TC
        XDBL = XDBLP[gseg % 2]
        mk_bcr = marks[("xdbl_evac", gseg)]
        oproj_guard = marks["ws_pe_done"]

        def pump(n):
            if fit is None:
                return
            for _ in range(n):
                if next(fit, "DONE") == "DONE":
                    break

        if pid == 1 and gseg == 0:
            P.wait("dve", "act", P.cnt["act"])
            P.op("dve", lambda v: v.memset(SCAR[:], 0.0))
            mk_rst = P.mark("dve")
            P.wait("act", "dve", mk_rst)
        if pid == 0:
            P.wait("pool", "dve", marks["xin_done"])
            P.wait("pool", "dve", marks["oproj_evac_dve"])
            P.wait("pool", "act", marks["stats_evac_act"])
            P.wait("pool", "pe", marks[("xdbl_mm", gseg)])
            P.wait("pool", "dxm", marks[("xm_spilled", gseg)])
            P.wait("pool", "dsl", marks[("sil_spilled", gseg)])
            for k in range(DT_K):
                P.op("pool", lambda p_, kk=k: p_.tensor_tensor(
                    FSCR[:, 0:TC], XSEG[:, kk * TC:(kk + 1) * TC],
                    MBX[:, 0:TC], ALU.subtract))
                P.op("pool", lambda p_, kk=k: p_.tensor_tensor(
                    H0B[:, kk * TC:(kk + 1) * TC], FSCR[:, 0:TC],
                    SBX[:, 0:TC], ALU.mult))
            marks["h0b_done_pool"] = P.mark("pool")
        P.wait("pool", "act", marks["il_done"])
        P.wait("sp", "dxm", marks[("xm_spilled", gseg)])
        P.wait("sp", "dsl", marks[("sil_spilled", gseg)])
        hist = marks.setdefault("hist", {})

        def emit_ld(i):
            P.wait("sp", "dve", marks.get(("gate_par", i % 2)))
            p_ = i % 2
            marks[("ldm", i)] = P.dma(
                "sp", lambda e, ii=i, tt=t0, pp=p_: e.dma_start(
                    out=XMD[:, pp * TC:(pp + 1) * TC],
                    in_=xm_sp[ii * 128:(ii + 1) * 128, tt:tt + TC]), "dxt")
            marks[("lds", i)] = P.dma(
                "sp", lambda e, ii=i, tt=t0, pp=p_: e.dma_start(
                    out=SLD[:, pp * TC:(pp + 1) * TC],
                    in_=sil_sp[ii * 128:(ii + 1) * 128, tt:tt + TC]), "dst")

        def emit_dt_sp(i):
            dl = DELTA0 if i % 2 == 0 else DELTA1
            P.wait("pe", "act", mk_bcr)
            P.wait("pe", "act", marks["last_sp_act"])
            MM(ps[5][:, 0:TC], DTW[:, i * 128:(i + 1) * 128], XDBL[:, 0:TC],
               True, True)
            mk = P.mark("pe")
            P.wait("act", "pe", mk)
            P.wait("act", "dve", marks.get(("dadd_mk", i - 1)))
            ACT(dl, ps[5][:, 0:TC], AF.Relu, bias=DTB[:, i:i + 1])
            ACT(SPT, ps[5][:, 0:TC], AF.Abs, bias=DTB[:, i:i + 1])
            ACT(SPT, SPT, AF.Exp, scale=-1.0)
            ACT(SPT, SPT, AF.Ln, bias=1.0)
            mk_sp = P.mark("act")
            marks["last_sp_act"] = mk_sp
            return mk_sp

        def emit_dadd_du(i, mk_sp):
            dl = DELTA0 if i % 2 == 0 else DELTA1
            du = DU0 if i % 2 == 0 else DU1
            P.wait("dve", "act", mk_sp)
            TT(dl, dl, SPT, ALU.add)
            marks[("dadd_mk", i)] = P.mark("dve")
            P.wait("dve", "dxt", marks.get(("ldm", i)))
            P.wait("dve", "pool", marks.get(("dbu_done", i - 2)))
            TT(du, dl, XMD[:, (i % 2) * TC:(i % 2 + 1) * TC], ALU.mult)
            return P.mark("dve")

        def emit_dbu_half(i, mk_du, h):
            du = DU0 if i % 2 == 0 else DU1
            duv = du.unsqueeze(2).broadcast_to([128, TC, 8])
            P.wait("pool", "dve", mk_du)
            P.wait("pool", "dve", hist.get(("war", h)))
            P.op("pool", lambda p_, dv=duv, hh=h: p_.tensor_tensor(
                DBUIL[:, hh * TC * 8:(hh + 1) * TC * 8].rearrange(
                    "p (t s) -> p t s", s=8),
                dv,
                BIL[:].rearrange("p (t s) -> p t s", s=16)
                [:, :, hh * 8:(hh + 1) * 8], ALU.mult))
            hist[("dbu", h)] = P.mark("pool")
            if h == 1:
                marks[("dbu_done", i)] = hist[("dbu", 1)]

        emit_ld(0)
        emit_ld(1)
        mk_spc = emit_dt_sp(0)
        mk_duc = emit_dadd_du(0, mk_spc)
        emit_dbu_half(0, mk_duc, 0)
        emit_dbu_half(0, mk_duc, 1)
        mk_y = None
        for i in range(16):
            if i + 1 < 16:
                mk_spn = emit_dt_sp(i + 1)
                mk_dun = emit_dadd_du(i + 1, mk_spn)
                marks[("du_mk", i + 1)] = mk_dun
            if i > 0:
                emit_dbu_half(i, marks[("du_mk", i)], 1)
            dl = DELTA0 if i % 2 == 0 else DELTA1
            P.wait("act", "dve", marks.get(("dadd_mk", i)))
            for s in range(16):
                dab = DA0 if s % 2 == 0 else DA1
                P.wait("act", "dve", hist.get((s - 2) % 16))
                ACT(dab, dl, AF.Exp,
                    scale=ASB[:, 16 * i + s:16 * i + s + 1])
                mk_da = P.mark("act")
                P.wait("dve", "act", mk_da)
                P.wait("dve", "pool", hist[("dbu", s // 8)])
                P.op("dve", lambda v, ss=s, db=dab, ii=i:
                     v.tensor_tensor_scan(
                         HIL[:].rearrange("p (t s) -> p t s", s=16)
                         [:, :, ss],
                         db,
                         DBUIL[:, (ss // 8) * TC * 8:(ss // 8 + 1) * TC * 8]
                         .rearrange("p (t s) -> p t s", s=8)[:, :, ss % 8],
                         SCAR[:, 16 * ii + ss:16 * ii + ss + 1],
                         ALU.mult, ALU.add))
                mk_scan = P.mark("dve")
                hist[s] = mk_scan
                if s == 7:
                    hist[("war", 0)] = mk_scan
                elif s == 15:
                    hist[("war", 1)] = mk_scan
            if i + 1 < 16:
                emit_dbu_half(i + 1, mk_dun, 0)
            P.wait("act", "dve", hist[15])
            ACT(SCAR[:, 16 * i:16 * i + 16],
                HIL[:, (TC - 1) * 16:(TC - 1) * 16 + 16], AF.Copy)
            P.mark("act")
            ym_mk = {}
            for th in range(4):
                P.wait("pool", "dve", hist[15])
                P.op("pool", lambda p_, tt=th: p_.tensor_tensor(
                    YPB[:].rearrange("p (t s) -> p t s", s=16)
                    [:, tt * 128:(tt + 1) * 128, :],
                    HIL[:].rearrange("p (t s) -> p t s", s=16)
                    [:, tt * 128:(tt + 1) * 128, :],
                    CIL[:].rearrange("p (t s) -> p t s", s=16)
                    [:, tt * 128:(tt + 1) * 128, :], ALU.mult))
                ym_mk[th] = P.mark("pool")
            pump(3)
            for th in range(4):
                P.wait("dve", "pool", ym_mk[th])
                P.op("dve", lambda v, tt=th: v.tensor_reduce(
                    YSSM[:, tt * 128:(tt + 1) * 128],
                    YPB[:].rearrange("p (t s) -> p t s", s=16)
                    [:, tt * 128:(tt + 1) * 128, :],
                    axis=AX.X, op=ALU.add))
            P.wait("dve", "dst", marks.get(("lds", i)))
            par = i % 2
            STT(YSSM, XMD[:, par * TC:(par + 1) * TC], DSSM[:, i:i + 1],
                YSSM, ALU.mult, ALU.add)
            if i == 0:
                P.wait("dve", "pe", oproj_guard)
            TT(YBF[:, i * TC:(i + 1) * TC], YSSM,
               SLD[:, par * TC:(par + 1) * TC], ALU.mult)
            mk_y = P.mark("dve")
            marks[("gate_par", i % 2)] = mk_y
            if i + 2 < 16:
                emit_ld(i + 2)
        marks["bil_consumed_pool"] = P.cnt["pool"]
        marks["scan_gate_done"] = mk_y
        pump(100)

    def emit_oproj(gseg, pid):
        t0 = gseg * TC
        opw_v = opw_bf.rearrange("(k p) n -> p k n", p=128)
        spill_sem = "dsp1" if pid == 0 else "dsp3"
        dst_sp = x1_sp if pid == 0 else out2_sp
        slot_spill = {0: None, 1: None}
        bank_evac = {}
        ev_act, ev_dve = None, None
        P.wait("pe", "dve", marks["scan_gate_done"])
        P.wait("pe", "act", marks["inproj_evac_act"])
        P.wait("pe", "act", marks["stats_evac_act"])
        for half in range(2):
            P.wait("sp", "pe", marks["ws_pe_done"])
            ldop = P.dma("sp", lambda e, hh=half: e.dma_start(
                out=WS[:].rearrange("p (k n) -> p k n", n=512),
                in_=opw_v[:, :, hh * 512:(hh + 1) * 512]), "dW")
            P.wait("pe", "dW", ldop)
            for m4 in range(4):
                m = half * 4 + m4
                pb = ps[m4]
                if m4 in bank_evac:
                    P.wait("pe", bank_evac[m4][0], bank_evac[m4][1])
                for k8 in range(16):
                    MM(pb[:, 0:TC],
                       WS[:, k8 * 512 + m4 * 128:k8 * 512 + m4 * 128 + 128],
                       YBF[:, k8 * TC:(k8 + 1) * TC], k8 == 0, k8 == 15)
                mk_mm = P.mark("pe")
                slot = OPSL[:, (m % 2) * TC:(m % 2 + 1) * TC]
                if pid == 0:
                    P.wait("dve", "pe", mk_mm)
                    P.wait("dve", "pool", marks["h0b_done_pool"])
                    P.wait("dve", spill_sem, slot_spill[m % 2])
                    STT(slot, pb[:, 0:TC], OPB[:, m:m + 1],
                        H0B[:, m * TC:(m + 1) * TC], ALU.add, ALU.add)
                    ev_dve = P.mark("dve")
                    bank_evac[m4] = ("dve", ev_dve)
                    P.wait("sp", "dve", ev_dve)
                else:
                    P.wait("act", "pe", mk_mm)
                    P.wait("act", spill_sem, slot_spill[m % 2])
                    ACT(slot, pb[:, 0:TC], AF.Identity, bias=OPB[:, m:m + 1])
                    ev_act = P.mark("act")
                    bank_evac[m4] = ("act", ev_act)
                    P.wait("sp", "act", ev_act)
                slot_spill[m % 2] = P.dma(
                    "sp", lambda e, mm=m, tt=t0, sl=slot: e.dma_start(
                        out=dst_sp[mm * 128:(mm + 1) * 128, tt:tt + TC],
                        in_=sl), spill_sem)
            marks["ws_pe_done"] = P.cnt["pe"]
        marks["oproj_spill"][spill_sem] = slot_spill[1]
        if ev_act is not None:
            marks["oproj_evac_act"] = ev_act
        if ev_dve is not None:
            marks["oproj_evac_dve"] = ev_dve
        if pid == 0:
            marks[("x1_seg", gseg)] = P.dma_cnt["dsp1"]
            marks["x1_spill_total"] = P.dma_cnt["dsp1"]
        else:
            marks["out2_spill_total"] = P.dma_cnt["dsp3"]

    units = [(0, gg) for gg in range(NSEG)]
    if stop_after not in ("p1", "pro"):
        units += [(1, gg) for gg in range(NSEG)]
    if stop_after != "pro":
        fit0 = front_steps(0, 0)
        for _ in range(200):
            if next(fit0, "DONE") == "DONE":
                break
        finish_front(0, 0)
        for ui, (pid, gg) in enumerate(units):
            ckpt(f"m{pid}s{gg}_scan")
            emit_ffn_casts(12)
            nxt = units[ui + 1] if ui + 1 < len(units) else None
            fit = None
            if nxt is not None:
                fit = front_steps(
                    nxt[1], nxt[0],
                    first_pass_reset=(nxt[0] == 1 and nxt[1] == 0))
            emit_scan(gg, pid, fit)
            if nxt is not None:
                finish_front(nxt[1], nxt[0])
            ckpt(f"m{pid}s{gg}_oproj")
            emit_oproj(gg, pid)
    st_m.close()

    run_2b = stop_after in (None, "p2b", "p3")
    st_2b = _ctx.ExitStack()
    bar_pe, bar_act = P.cnt["pe"], P.cnt["act"]
    bar_dve, bar_pool = P.cnt["dve"], P.cnt["pool"]
    for w in ("sp", "dve"):
        for eng, cnt in (("pe", bar_pe), ("act", bar_act), ("dve", bar_dve),
                         ("pool", bar_pool)):
            if w != eng and cnt:
                P.wait(w, eng, cnt)
        for dsem in ("dsp1", "dsp2", "dsp3", "dxm", "dsl", "dxt", "dst",
                     "dW", "dX"):
            P.wait(w, dsem, P.dma_cnt.get(dsem))
    WWW = mksb(st_2b, "WWW", [128, DT_K * 1024], F32)
    WRW = mksb(st_2b, "WRW", [128, DT_K * 1024], F32)
    WT = mksb(st_2b, "WT", [128, DT_K * 1024], F32)
    O2SEG = mksb(st_2b, "O2SEG", [128, DT_K * (TC + 1)], F32)
    VTM = mksb(st_2b, "VTM", [128, DT_K * 128], F32)
    VGTM = mksb(st_2b, "VGTM", [128, DT_K * 128], F32)
    WKTM = mksb(st_2b, "WKTM", [128, DT_K * 128], F32)
    SMT = mksb(st_2b, "SMT", [128, 128], F32)
    RKG = mksb(st_2b, "RKG", [128, DT_K * 128], F32)
    RDB = mksb(st_2b, "RDB", [128, DT_K * TC], F32)
    ACC2 = mksb(st_2b, "ACC2", [128, DT_K * TC], F32)
    X2SEG = mksb(st_2b, "X2SEG", [128, DT_K * TC], F32)
    P.op("dve", lambda v: v.memset(WT[:], 0.0))
    for k in range(DT_K):
        P.op("dve", lambda v, kk=k: v.memset(
            O2SEG[:, kk * (TC + 1):kk * (TC + 1) + 1], 0.0))
    mk_zero = P.mark("dve")
    P.dma("sp", lambda e: e.dma_start(
        out=WWW[:].rearrange("p (k n) -> p k n", n=1024),
        in_=wwt.rearrange("(k p) n -> p k n", p=128)), "dwr")
    ldw = P.dma("sp", lambda e: e.dma_start(
        out=WRW[:].rearrange("p (k n) -> p k n", n=1024),
        in_=wrt.rearrange("(k p) n -> p k n", p=128)), "dwr")
    P.wait("pe", "dwr", ldw)
    P.wait("dve", "act", mk_ginr)

    mk_lead = mk_zero
    mk_xa_dve = None
    for g in range(NSEG if run_2b else 0):
        ckpt(f"2b_s{g}")
        t0 = g * TC
        P.wait("sp", "dve", mk_lead)
        P.wait("sp", "dve", mk_xa_dve)
        P.wait("sp", "dsp3", marks.get("out2_spill_total"))
        ldo = P.dma("sp", lambda e, tt=t0: e.dma_start(
            out=O2SEG[:].rearrange("p (k n) -> p k n", n=TC + 1)[:, :, 1:TC + 1],
            in_=out2_sp.rearrange("(k p) n -> p k n", p=128)[:, :, tt:tt + TC]),
            "dO")
        P.wait("sp", "dsp1", marks.get("x1_spill_total"))
        ldacc = P.dma("sp", lambda e, tt=t0: e.dma_start(
            out=ACC2[:].rearrange("p (k n) -> p k n", n=TC),
            in_=x1_sp.rearrange("(k p) n -> p k n", p=128)[:, :, tt:tt + TC]),
            "dA2")
        ldxx = P.dma("sp", lambda e, tt=t0: e.dma_start(
            out=X2SEG[:].rearrange("p (k n) -> p k n", n=TC),
            in_=xT.rearrange("(k p) n -> p k n", p=128)[:, :, tt:tt + TC]),
            "dX2")
        P.wait("pe", "dO", ldo)

        def o2(k, c0, c1):
            return O2SEG[:, k * (TC + 1) + c0:k * (TC + 1) + c1]

        for c in range(NCH):
            cc = c * CH
            for nh in range(2):
                pb = ps[0 + nh]
                for k in range(DT_K):
                    MM(pb[:, 0:512], o2(k, cc + 1, cc + CH + 1),
                       WWW[:, k * 1024 + nh * 512:k * 1024 + (nh + 1) * 512],
                       k == 0, k == DT_K - 1)
            mkv = P.mark("pe")
            P.wait("act", "pe", mkv)
            for nh in range(2):
                ACT(VTM[:, nh * 512:(nh + 1) * 512], ps[nh][:, 0:512], AF.Copy)
                ACT(VGTM[:, nh * 512:(nh + 1) * 512], ps[nh][:, 0:512],
                    AF.Identity, scale=GW[:, 0:1])
            mk_v = P.mark("act")
            P.wait("pe", "act", mk_v)
            mk_te = None
            for k in range(DT_K):
                P.op("pe", lambda t, kk=k, c0=cc: t.transpose(
                    ps[2][:, (kk % 2) * 128:(kk % 2) * 128 + 128],
                    o2(kk, c0, c0 + CH), EYE[:]))
                mkT = P.mark("pe")
                P.wait("act", "pe", mkT)
                ACT(WKTM[:, k * 128:(k + 1) * 128],
                    ps[2][:, (k % 2) * 128:(k % 2) * 128 + 128], AF.Copy)
                mk_te = P.mark("act")
                P.wait("pe", "act", mk_te)
            for k in range(DT_K):
                MM(ps[3][:, 0:128], o2(k, cc, cc + CH), o2(k, cc + 1, cc + CH + 1),
                   k == 0, k == DT_K - 1)
            mk_st = P.mark("pe")
            P.wait("dve", "pe", mk_st)
            TT(SMT[:], ps[3][:, 0:128], MT[:], ALU.mult)
            for k in range(DT_K):
                TT(RKG[:, k * 128:(k + 1) * 128], o2(k, cc + 1, cc + CH + 1),
                   GINR[:], ALU.mult)
            mk_rkg = P.mark("dve")
            P.wait("pe", "dve", mk_rkg)
            for m in range(DT_K):
                pb = ps[4 + (m % 2)]
                col = ((m // 2) % 2) * 128
                for k in range(DT_K):
                    MM(pb[:, col:col + 128],
                       WT[:, k * 1024 + m * 128:k * 1024 + m * 128 + 128],
                       RKG[:, k * 128:(k + 1) * 128], k == 0, False)
                MM(pb[:, col:col + 128], VTM[:, m * 128:(m + 1) * 128], SMT[:],
                   False, True)
                mk_rd = P.mark("pe")
                P.wait("act", "pe", mk_rd)
                ACT(RDB[:, m * TC + cc:m * TC + cc + CH], pb[:, col:col + 128],
                    AF.Copy)
                mk_rde = P.mark("act")
                P.wait("pe", "act", mk_rde)
            mk_wb_hist = {}
            for m in range(DT_K):
                for nh in range(2):
                    j = m * 2 + nh
                    pb = ps[6] if j % 2 == 0 else ps[3]
                    P.wait("pe", "dve", mk_wb_hist.get(j - 2))
                    MM(pb[:, 0:512], WKTM[:, m * 128:(m + 1) * 128],
                       VGTM[:, nh * 512:(nh + 1) * 512], True, True)
                    mk_wu = P.mark("pe")
                    P.wait("dve", "pe", mk_wu)
                    STT(WT[:, m * 1024 + nh * 512:m * 1024 + (nh + 1) * 512],
                        WT[:, m * 1024 + nh * 512:m * 1024 + (nh + 1) * 512],
                        GC[:, 0:1], pb[:, 0:512], ALU.mult, ALU.add)
                    mk_wb_hist[j] = P.mark("dve")
        P.op("dve", lambda v: v.tensor_copy(
            O2SEG[:].rearrange("p (k n) -> p k n", n=TC + 1)[:, :, 0:1],
            O2SEG[:].rearrange("p (k n) -> p k n", n=TC + 1)[:, :, TC:TC + 1]))
        mk_lead = P.mark("dve")
        P.wait("pe", "dve", mk_lead)
        mk_xa_l = []
        for m in range(DT_K):
            pb = ps[m % 4]
            if m >= 4:
                P.wait("pe", "dve", mk_xa_l[m - 4])
            for k in range(DT_K):
                MM(pb[:, 0:TC],
                   WRW[:, k * 1024 + m * 128:k * 1024 + m * 128 + 128],
                   RDB[:, k * TC:(k + 1) * TC], k == 0, k == DT_K - 1)
            mk_rp = P.mark("pe")
            P.wait("dve", "pe", mk_rp)
            P.wait("dve", "dA2", ldacc)
            P.wait("dve", "dX2", ldxx)
            STT(ACC2[:, m * TC:(m + 1) * TC], pb[:, 0:TC], ALPHA,
                ACC2[:, m * TC:(m + 1) * TC], ALU.mult, ALU.add)
            TT(ACC2[:, m * TC:(m + 1) * TC], ACC2[:, m * TC:(m + 1) * TC],
               X2SEG[:, m * TC:(m + 1) * TC], ALU.add)
            TT(ACC2[:, m * TC:(m + 1) * TC], ACC2[:, m * TC:(m + 1) * TC],
               O2SEG[:, m * (TC + 1) + 1:m * (TC + 1) + TC + 1], ALU.add)
            mk_xa_l.append(P.mark("dve"))
        mk_xa_dve = mk_xa_l[-1]
        P.wait("sp", "dve", mk_xa_dve)
        marks["xa_spill_total"] = P.dma(
            "sp", lambda e, tt=t0: e.dma_start(
                out=xA_sp.rearrange("(k p) n -> p k n", p=128)
                [:, :, tt:tt + TC],
                in_=ACC2[:].rearrange("p (k n) -> p k n", n=TC)), "dxa")

    st_2b.close()
    emit_ffn_casts(len(ffn_cast_chunks))
    P.wait("sp", "ds0", P.dma_cnt.get("ds0"))
    P.wait("sp", "ds1", P.dma_cnt.get("ds1"))
    st_pro.close()
    run_3 = stop_after in (None, "p3")
    bar_pe, bar_act = P.cnt["pe"], P.cnt["act"]
    bar_dve, bar_pool = P.cnt["dve"], P.cnt["pool"]
    for w in ("sp", "act", "dve"):
        for eng, cnt in (("pe", bar_pe), ("act", bar_act), ("dve", bar_dve),
                         ("pool", bar_pool)):
            if w != eng and cnt:
                P.wait(w, eng, cnt)
        for dsem in ("dO", "dA2", "dX2", "dxa", "dwr"):
            P.wait(w, dsem, P.dma_cnt.get(dsem))
    st_3 = _ctx.ExitStack()
    GBUF = mksb(st_3, "GBUF", [128, HF_K * TB], BF16)
    HOUT = mksb(st_3, "HOUT", [128, HF_K * TB], BF16)
    XAB = mksb(st_3, "XAB", [128, DT_K * TB], F32)
    XHB = mksb(st_3, "XHB", [128, DT_K * TB], BF16)
    WSL = mksb(st_3, "WSL", [128, HF_K * 128], BF16)
    _wf = WSL[:].bitcast(F32)
    MBF = _wf[:, 0:TB]
    SBF = _wf[:, TB:2 * TB]
    _gf = GBUF[:].bitcast(F32)
    MRW = _gf[0:1, 0:TB]
    VRW = _gf[0:1, TB:2 * TB]
    WSL2 = HOUT[:, 4 * TB:5 * TB]
    out_done = None
    sq = HOUT[:, 0:4 * TB].bitcast(F32)
    mk_p3_dve = mk_xa_dve
    for b in range(NTB if run_3 else 0):
        ckpt(f"ffn_b{b}")
        t0 = b * TB
        P.wait("sp", "dve", mk_p3_dve)
        P.wait("sp", "dxa", marks.get("xa_spill_total"))
        ldxa = P.dma("sp", lambda e, tt=t0: e.dma_start(
            out=XAB[:].rearrange("p (k n) -> p k n", n=TB),
            in_=xA_sp.rearrange("(k p) n -> p k n", p=128)[:, :, tt:tt + TB]),
            "dX3")
        P.wait("act", "dX3", ldxa)
        P.wait("pe", "dX3", ldxa)
        for k in range(DT_K):
            ACT(sq[:, 0:TB], XAB[:, k * TB:(k + 1) * TB], AF.Square)
            mk_sq = P.mark("act")
            P.wait("pe", "act", mk_sq)
            MM(ps[0][0:1, 0:512], ONESK[:], XAB[:, k * TB:k * TB + 512],
               k == 0, k == DT_K - 1)
            MM(ps[1][0:1, 0:512], ONESK[:], XAB[:, k * TB + 512:(k + 1) * TB],
               k == 0, k == DT_K - 1)
            MM(ps[2][0:1, 0:512], ONESK[:], sq[:, 0:512], k == 0,
               k == DT_K - 1)
            MM(ps[3][0:1, 0:512], ONESK[:], sq[:, 512:TB], k == 0,
               k == DT_K - 1)
            mk_smm = P.mark("pe")
            P.wait("act", "pe", mk_smm)
        mk_sums = P.mark("pe") if False else mk_smm
        P.wait("act", "pe", mk_sums)
        ACT(MRW[0:1, 0:512], ps[0][0:1, 0:512], AF.Copy, scale=1.0 / D)
        ACT(MRW[0:1, 512:TB], ps[1][0:1, 0:512], AF.Copy, scale=1.0 / D)
        ACT(VRW[0:1, 0:512], ps[2][0:1, 0:512], AF.Copy, scale=1.0 / D)
        ACT(VRW[0:1, 512:TB], ps[3][0:1, 0:512], AF.Copy, scale=1.0 / D)
        ACT(sq[0:1, 0:TB], MRW[0:1, 0:TB], AF.Square)
        mk_v = P.mark("act")
        P.wait("dve", "act", mk_v)
        TT(VRW[0:1, 0:TB], VRW[0:1, 0:TB], sq[0:1, 0:TB], ALU.subtract)
        mk_v2 = P.mark("dve")
        P.wait("act", "dve", mk_v2)
        ACT(VRW[0:1, 0:TB], VRW[0:1, 0:TB], AF.Ln, bias=EPS)
        ACT(VRW[0:1, 0:TB], VRW[0:1, 0:TB], AF.Exp, scale=-0.5)
        mk_istd = P.mark("act")
        P.wait("pe", "act", mk_istd)
        MM(ps[0][:, 0:512], ONES1[:], MRW[0:1, 0:512], True, True)
        MM(ps[1][:, 0:512], ONES1[:], MRW[0:1, 512:TB], True, True)
        MM(ps[2][:, 0:512], ONES1[:], VRW[0:1, 0:512], True, True)
        MM(ps[3][:, 0:512], ONES1[:], VRW[0:1, 512:TB], True, True)
        mk_bc = P.mark("pe")
        P.wait("act", "pe", mk_bc)
        ACT(MBF[:, 0:512], ps[0][:, 0:512], AF.Copy)
        ACT(MBF[:, 512:TB], ps[1][:, 0:512], AF.Copy)
        ACT(SBF[:, 0:512], ps[2][:, 0:512], AF.Copy)
        ACT(SBF[:, 512:TB], ps[3][:, 0:512], AF.Copy)
        mk_bce = P.mark("act")
        P.wait("dve", "act", mk_bce)
        for k in range(DT_K):
            TT(sq[:, 0:TB], XAB[:, k * TB:(k + 1) * TB], MBF[:, 0:TB],
               ALU.subtract)
            TT(XHB[:, k * TB:(k + 1) * TB], sq[:, 0:TB], SBF[:, 0:TB],
               ALU.mult)
        mk_xh = P.mark("dve")
        P.wait("pe", "dve", mk_xh)
        P.wait("act", "dve", mk_xh)

        def wslice_dma(wsrc, m, buf, kt, after_pe=None, sem="dwA"):
            P.wait("sp", "pe", after_pe)
            if buf is WSL2:
                bap = buf
            else:
                bap = buf[:]
            return P.dma("sp", lambda e, s=wsrc, mm=m, b=bap, K=kt:
                         e.dma_start(
                             out=b.rearrange("p (k n) -> p k n", n=128)
                             [:, 0:K, :],
                             in_=s.rearrange("(k p) n -> p k n", p=128)
                             [:, 0:K, mm * 128:(mm + 1) * 128]), sem)

        P.wait("sp", "dve", mk_xh)
        mk_gq = {}

        def w12_ld(m):
            q = (m % 2) * 2
            P.wait("sp", "pe", mk_gq.get(m - 2))
            l1 = P.dma("sp", lambda e, mm=m, qq=q: e.dma_start(
                out=WSL[:, qq * 1024:(qq + 1) * 1024].rearrange(
                    "p (k n) -> p k n", n=128),
                in_=w1_bf.rearrange("(k p) n -> p k n", p=128)
                [:, 0:DT_K, mm * 128:(mm + 1) * 128]), "dwA")
            l2 = P.dma("sp", lambda e, mm=m, qq=q: e.dma_start(
                out=WSL[:, (qq + 1) * 1024:(qq + 2) * 1024].rearrange(
                    "p (k n) -> p k n", n=128),
                in_=w2_bf.rearrange("(k p) n -> p k n", p=128)
                [:, 0:DT_K, mm * 128:(mm + 1) * 128]), "dwB")
            return (l1, l2)

        lw = {0: w12_ld(0)}
        for m in range(HF_K):
            if m + 1 < HF_K:
                lw[m + 1] = w12_ld(m + 1)
            q = (m % 2) * 2
            P.wait("pe", "dwA", lw[m][0])
            P.wait("pe", "dwB", lw[m][1])
            for k in range(DT_K):
                for nh in range(2):
                    MM(ps[0 + nh][:, 0:512],
                       WSL[:, q * 1024 + k * 128:q * 1024 + (k + 1) * 128],
                       XHB[:, k * TB + nh * 512:k * TB + (nh + 1) * 512],
                       k == 0, k == DT_K - 1)
            for k in range(DT_K):
                for nh in range(2):
                    MM(ps[2 + nh][:, 0:512],
                       WSL[:, (q + 1) * 1024 + k * 128:
                           (q + 1) * 1024 + (k + 1) * 128],
                       XHB[:, k * TB + nh * 512:k * TB + (nh + 1) * 512],
                       k == 0, k == DT_K - 1)
            mk_g = P.mark("pe")
            mk_gq[m] = mk_g
            if False:
                pass
            P.wait("act", "pe", mk_g)
            ACT(sq[:, 0:512], ps[0][:, 0:512], AF.Silu, scale=FS1[:, m:m + 1])
            ACT(sq[:, 512:TB], ps[1][:, 0:512], AF.Silu, scale=FS1[:, m:m + 1])
            ACT(sq[:, TB:TB + 512], ps[2][:, 0:512], AF.Identity,
                scale=FS2[:, m:m + 1])
            ACT(sq[:, TB + 512:2 * TB], ps[3][:, 0:512], AF.Identity,
                scale=FS2[:, m:m + 1])
            mk_t = P.mark("act")
            P.wait("dve", "act", mk_t)
            TT(GBUF[:, m * TB:(m + 1) * TB], sq[:, 0:TB], sq[:, TB:2 * TB],
               ALU.mult)
            mk_gd = P.mark("dve")
            P.wait("pe", "dve", mk_gd)
            P.wait("act", "dve", mk_gd)
        mk_h = P.mark("pe") if False else P.cnt["pe"]
        P.wait("sp", "dve", mk_gd)

        WSLB = XHB[:, 0:HF_K * 128]
        mk_evac = {}

        def stage(wsrc, out_fn, n_m, rd_buf, scale_tab, last_stage=False):
            nonlocal mk_h
            lds = {0: wslice_dma(wsrc, 0, WSL, HF_K, after_pe=mk_h,
                                 sem="dwA")}
            lds[1] = wslice_dma(wsrc, 1, WSLB, HF_K, after_pe=mk_h,
                                sem="dwB") if n_m > 1 else None
            for m in range(n_m):
                wbuf = WSL if m % 2 == 0 else WSLB
                wsem = "dwA" if m % 2 == 0 else "dwB"
                pban = (m % 2) * 2
                P.wait("pe", wsem, lds[m])
                P.wait("pe", "act" if not last_stage else "dve",
                       mk_evac.get(m - 2))
                for nh in range(2):
                    for k in range(HF_K):
                        MM(ps[pban + nh][:, 0:512],
                           wbuf[:, k * 128:(k + 1) * 128],
                           rd_buf[:, k * TB + nh * 512:k * TB + (nh + 1) * 512],
                           k == 0, k == HF_K - 1)
                mk_h = P.mark("pe")
                if m + 2 < n_m:
                    lds[m + 2] = wslice_dma(
                        wsrc, m + 2, WSL if m % 2 == 0 else WSLB, HF_K,
                        after_pe=mk_h, sem=wsem)
                out_fn(m, pban)
                mk_evac[m] = P.mark("act" if not last_stage else "dve")

        def h1_out(m, pban):
            P.wait("act", "pe", mk_h)
            ACT(HOUT[:, m * TB:m * TB + 512], ps[pban][:, 0:512], AF.Gelu,
                scale=FH1[:, m:m + 1])
            ACT(HOUT[:, m * TB + 512:(m + 1) * TB], ps[pban + 1][:, 0:512],
                AF.Gelu, scale=FH1[:, m:m + 1])

        stage(h1_bf, h1_out, HF_K, GBUF, FH1)

        def h2_out(m, pban):
            P.wait("act", "pe", mk_h)
            ACT(GBUF[:, m * TB:m * TB + 512], ps[pban][:, 0:512], AF.Gelu,
                scale=FH2[:, m:m + 1])
            ACT(GBUF[:, m * TB + 512:(m + 1) * TB], ps[pban + 1][:, 0:512],
                AF.Gelu, scale=FH2[:, m:m + 1])

        mk_evac = {}
        stage(h2_bf, h2_out, HF_K, HOUT, FH2)
        mk_yo = None

        def w3_out(m, pban):
            P.wait("dve", "pe", mk_h)
            STT(XAB[:, m * TB:m * TB + 512], ps[pban][:, 0:512],
                FS3[:, m:m + 1], XAB[:, m * TB:m * TB + 512],
                ALU.mult, ALU.add)
            STT(XAB[:, m * TB + 512:(m + 1) * TB], ps[pban + 1][:, 0:512],
                FS3[:, m:m + 1], XAB[:, m * TB + 512:(m + 1) * TB],
                ALU.mult, ALU.add)

        mk_evac = {}
        stage(w3_bf, w3_out, DT_K, GBUF, FS3, last_stage=True)
        mk_yo = P.cnt["dve"]
        mk_p3_dve = mk_yo
        P.wait("sp", "dve", mk_yo)
        out_done = P.dma("sp", lambda e, tt=t0: e.dma_start(
            out=yT.rearrange("(k p) n -> p k n", p=128)[:, :, tt:tt + TB],
            in_=XAB[:].rearrange("p (k n) -> p k n", n=TB)), "dy")

    for name, dout in dbg_outs.items():
        srct = dbg_srcs[name]
        out_done = P.dma("sp", lambda e, d=dout, s=srct: e.dma_start(
            out=d, in_=s), "dy")

    if out_done is not None:
        P.wait("sp", "dy", out_done)
    for _sem, _cnt in sorted(P.dma_cnt.items()):
        P.wait("sp", _sem, _cnt)
    st_3.close()

    with stack:
        semobjs = {}
        for s_ in sorted(P.sem_names):
            semobjs[s_] = stack.enter_context(nc.semaphore(s_))
        block = stack.enter_context(nc.Block())
        P.emit(nc, block, semobjs)
    return nc


def _pack_feat(v, ntiles):
    return np.ascontiguousarray(np.asarray(v, dtype=np.float32)
                                .reshape(ntiles, 128).T)


def make_inputs(core, I):
    f32 = np.float32
    xb = np.ascontiguousarray(np.asarray(I["L0_attn_x"][core], dtype=f32).T)
    selB = np.zeros((32, 16 * 128), dtype=ml_dtypes.bfloat16)
    selC = np.zeros((32, 16 * 128), dtype=ml_dtypes.bfloat16)
    for s in range(16):
        selB[s, s * 128:(s + 1) * 128] = 1
        selC[16 + s, s * 128:(s + 1) * 128] = 1
    d = {
        "xT": xb,
        "ipw": np.ascontiguousarray(np.asarray(I["L0_attn_in_proj_w"], f32).T),
        "xpw": np.ascontiguousarray(np.asarray(I["L0_attn_x_proj_w"], f32).T),
        "dtw": np.ascontiguousarray(np.asarray(I["L0_attn_dt_proj_w"], f32).T),
        "opw": np.ascontiguousarray(np.asarray(I["L0_attn_out_proj_w"], f32).T),
        "wwt": np.ascontiguousarray(np.asarray(I["L0_attn_proj_write_w"], f32).T),
        "wrt": np.ascontiguousarray(np.asarray(I["L0_attn_proj_read_w"], f32).T),
        "w1t": np.ascontiguousarray(np.asarray(I["L0_ffn_w1_w"], f32).T),
        "w2t": np.ascontiguousarray(np.asarray(I["L0_ffn_w2_w"], f32).T),
        "h1t": np.ascontiguousarray(np.asarray(I["L0_ffn_h1_w"], f32).T),
        "h2t": np.ascontiguousarray(np.asarray(I["L0_ffn_h2_w"], f32).T),
        "w3t": np.ascontiguousarray(np.asarray(I["L0_ffn_w3_w"], f32).T),
        "ipb_h": _pack_feat(I["L0_attn_in_proj_b"], 32),
        "convw_h": np.ascontiguousarray(
            np.asarray(I["L0_attn_conv_w"], f32).reshape(16, 128, 4)
            .transpose(1, 0, 2).reshape(128, 64)),
        "convb_h": _pack_feat(I["L0_attn_conv_b"], 16),
        "dtb_h": _pack_feat(I["L0_attn_dt_proj_b"], 16),
        "Alog_h": np.ascontiguousarray(
            np.asarray(I["L0_attn_A_log"], f32).reshape(16, 128, 16)
            .transpose(1, 0, 2).reshape(128, 256)),
        "Dssm_h": _pack_feat(I["L0_attn_D_ssm"], 16),
        "opb_h": _pack_feat(I["L0_attn_out_proj_b"], 8),
        "n1_h": _pack_feat(I["L0_attn_norm1_w"], 8),
        "n2_h": _pack_feat(I["L0_attn_norm2_w"], 8),
        "w1s_h": _pack_feat(I["L0_ffn_w1_s"], 32),
        "w2s_h": _pack_feat(I["L0_ffn_w2_s"], 32),
        "h1s_h": _pack_feat(I["L0_ffn_h1_s"], 32),
        "h2s_h": _pack_feat(I["L0_ffn_h2_s"], 32),
        "w3s_h": _pack_feat(I["L0_ffn_w3_s"], 8),
        "decay_h": np.asarray(I["L0_attn_decay"], dtype=f32).reshape(1, 1),
        "ones1_h": np.ones((1, 128), f32),
        "onesk_h": np.ones((128, 1), f32),
        "iota_h": np.arange(128, dtype=f32).reshape(128, 1),
        "iotar_h": (127 - np.arange(128, dtype=f32)).reshape(128, 1),
        "midx_h": (np.arange(128)[None, :] - 1 -
                   np.arange(128)[:, None]).astype(f32),
        "caus_h": (np.arange(128)[None, :] >
                   np.arange(128)[:, None]).astype(f32),
        "eye_h": np.eye(128, dtype=f32),
        "eyebf_h": np.eye(128, dtype=ml_dtypes.bfloat16),
        "selB_h": selB,
        "selC_h": selC,
        "cvals_h": np.broadcast_to(
            np.array([1.0, -1.0, -0.5, 128.0, 1.0 / 1024, 1e-5, 0.1, 0.0],
                     dtype=f32), (128, 8)).copy(),
    }
    return d


def kernel(**inputs):
    nc = build_nc()
    in_maps = [make_inputs(c, inputs) for c in range(8)]
    res = run_bass_kernel_spmd(nc, in_maps, core_ids=list(range(8)))
    out = np.stack([np.asarray(r["yT"]).T for r in res.results], axis=0)
    return np.ascontiguousarray(out.astype(np.float32))
